# version 18
# baseline (speedup 1.0000x reference)
import numpy as np
import ml_dtypes
import concourse.bass as bass
import concourse.mybir as mybir
from concourse.bass_utils import run_bass_kernel_spmd

F32 = mybir.dt.float32
BF16 = mybir.dt.bfloat16
I32 = mybir.dt.int32
AF = mybir.ActivationFunctionType
ALU = mybir.AluOpType
AX = mybir.AxisListType

T = 2048
D = 1024
NP = 8
NE = 8
EI = 512
SI = 1024
NTT = 16
NDC = 8
NCORES = 8

TWO_PI = 2 * np.pi
C1 = float(np.float32(TWO_PI))
C2P = float(np.float32(np.float64(np.float32(TWO_PI)) - TWO_PI))
INV_C1 = float(np.float32(1.0 / np.float64(np.float32(TWO_PI)))) 
PI_F = float(np.float32(np.pi))
PI_CLAMP = float(np.float32(3.1415925))
SHIFT_SIN = 0.0
SHIFT_COS = float(np.float32(np.pi / 2))

PHASES = {"tables", "ln1", "attn", "wo", "ln2", "gate", "shared", "experts"}
AR_WORLD = NCORES
GATE_STOP = 99
DBG = {}


class Chain:
    def __init__(self, nc, name):
        self.sem = nc.alloc_semaphore(name)
        self.n = 0


class Emitter:

    ENGINES = ("t", "v", "s", "g", "y")
    SELF_SERIAL = ("v", "s", "g")

    def __init__(self, nc):
        self.nc = nc
        self.ch = {e: Chain(nc, "c" + e) for e in self.ENGINES}
        self.streams = {}
        self.cur = None
        self.eng = None
        self.wm = {}
        self.totals = None

    def stream(self, name):
        if name not in self.streams:
            self.streams[name] = Chain(self.nc, "d_" + name)
        return self.streams[name]

    def begin(self, cur, eng):
        self.cur = cur
        self.eng = eng
        self.wm = {}
        for c in self.ch.values():
            c.n = 0
        for c in self.streams.values():
            c.n = 0

    def end(self):
        t = {k: c.n for k, c in self.ch.items()}
        t.update({"s_" + k: c.n for k, c in self.streams.items()})
        if self.totals is None:
            self.totals = t
        else:
            assert t == self.totals, f"pass divergence: {t} vs {self.totals}"

    def wait(self, tok):
        if tok is None or tok is False:
            return
        sem, val = tok
        if val <= 0:
            return
        key = sem.num
        if self.wm.get(key, 0) >= val:
            return
        self.eng.wait_ge(sem, val)
        self.wm[key] = val

    def op(self, engine, deps, fn):
        ch = self.ch[engine]
        if self.cur == engine:
            if engine in self.SELF_SERIAL and ch.n > 0:
                self.wait((ch.sem, ch.n))
            for d in deps:
                self.wait(d)
            inst = fn(self.eng)
            inst.then_inc(ch.sem, 1)
        ch.n += 1
        return (ch.sem, ch.n)

    def dma(self, queue, deps, fn, stream=None):
        ch = self.stream(stream if stream is not None else queue)
        issuer = "g" if queue == "dg" else "y"
        if self.cur == issuer:
            for d in deps:
                self.wait(d)
            inst = fn(self.eng)
            inst.then_inc(ch.sem, 16)
        ch.n += 16
        return (ch.sem, ch.n)

    def gate(self):
        toks = [(self.ch[e].sem, self.ch[e].n) for e in self.ENGINES]
        for tk in toks:
            self.wait(tk)


def build_program():
    nc = bass.Bass("TRN2", target_bir_lowering=False, debug=False)

    din = {}
    def I(name, shape):
        din[name] = nc.dram_tensor(name, list(shape), F32, kind="ExternalInput").ap()
        return din[name]

    x_in = I("x", (T, D))
    wqT = I("wqT", (D, D)); wkT = I("wkT", (D, D)); wvT = I("wvT", (D, D)); woT = I("woT", (D, D))
    I("bqc", (128, 8)); I("bkc", (128, 8))
    I("bvr", (1, D)); I("bor", (1, D)); I("bvc", (128, 8))
    gwT = I("gwT", (D, NE)); I("lb", (1, NE))
    egwT = I("egwT", (NE, D, EI)); euwT = I("euwT", (NE, D, EI)); edwT = I("edwT", (NE, EI, D))
    sgwT = I("sgwT", (D, SI)); suwT = I("suwT", (D, SI)); sdwT = I("sdwT", (SI, D))

    out = nc.dram_tensor("out", [T, D], F32, kind="ExternalOutput").ap()
    ar_in = nc.dram_tensor("ar_in", [1, 16], F32).ap()
    ar_out = nc.dram_tensor("ar_out", [1, 16], F32).ap()

    def inl(name, arr):
        return nc.inline_tensor(np.ascontiguousarray(arr), name=name).ap()

    idn_bf_c = inl("idn_bf", np.eye(128, dtype=ml_dtypes.bfloat16))
    idn_f_c = inl("idn_f", np.eye(128, dtype=np.float32))
    perm_np = np.zeros((128, 128), dtype=ml_dtypes.bfloat16)
    for i in range(128):
        perm_np[i ^ 32, i] = 1.0
    perm_c = inl("perm_bf", perm_np)
    tri_c = inl("tri_bf", (np.arange(128)[None, :] >= np.arange(128)[:, None]).astype(ml_dtypes.bfloat16))
    ones1b_c = inl("ones1_bf", np.ones((1, 128), dtype=ml_dtypes.bfloat16))
    ones1f_c = inl("ones1_f", np.ones((1, 128), dtype=np.float32))
    onescb_c = inl("onescol_bf", np.ones((128, 1), dtype=ml_dtypes.bfloat16))
    onescf_c = inl("onescol_f", np.ones((128, 1), dtype=np.float32))
    onesblk_c = inl("onesblk_f", np.ones((128, 64), dtype=np.float32))
    signc_c = inl("signcol", np.where((np.arange(128) % 64) < 32, -1.0, 1.0).astype(np.float32).reshape(128, 1))
    jcol_c = inl("jcol", (np.arange(128) % 32).astype(np.float32).reshape(128, 1))
    tbase_c = inl("tbase", np.broadcast_to(np.arange(T, dtype=np.float32), (128, T)))
    c10k_c = inl("c10k", np.full((1, 1), 10000.0, dtype=np.float32))

    base = (nc.sbuf_base + 31) // 32 * 32
    LIMIT = base + nc.sbuf_bytes_remaining - 512

    def AT(name, shape, dtype, off):
        nbytes = int(np.prod(shape[1:])) * mybir.dt.size(dtype)
        assert base + off + nbytes <= LIMIT, f"{name}: {off}+{nbytes} over SBUF"
        return nc.alloc_sbuf_tensor_at(name, list(shape), dtype, offset=base + off)

    class Bump:
        def __init__(self, off):
            self.o = off
        def al(self, name, shape, dtype):
            t_ = AT(name, shape, dtype, self.o)
            self.o += (int(np.prod(shape[1:])) * mybir.dt.size(dtype) + 31) // 32 * 32
            return t_

    P = Bump(0)
    x_tm = P.al("x_tm", [128, NTT, D], F32)
    xn_fm = P.al("xn_fm", [128, NDC, T], BF16)
    bqc = P.al("bqc", [128, 8], F32)
    bkc = P.al("bkc", [128, 8], F32)
    bvc = P.al("bvc", [128, 8], F32)
    gw_sl = P.al("gw_sl", [128, NDC, 8], F32)
    lb_sb = P.al("lb_sb", [1, 8], F32)
    lb_rep = P.al("lb_rep", [128, 8], F32)
    idn_bf = P.al("idn_bf_s", [128, 128], BF16)
    idn_f = P.al("idn_f_s", [128, 128], F32)
    perm_s = P.al("perm_s", [128, 128], BF16)
    tri_s = P.al("tri_s", [128, 128], BF16)
    ones1b = P.al("ones1b", [1, 128], BF16)
    ones1f = P.al("ones1f", [1, 128], F32)
    onescb = P.al("onescb", [128, 1], BF16)
    onescf = P.al("onescf", [128, 1], F32)
    onesblk = P.al("onesblk", [128, 64], F32)
    signc = P.al("signc", [128, 1], F32)
    jcol = P.al("jcol", [128, 1], F32)
    c10k = P.al("c10k", [1, 1], F32)
    eps_c = P.al("eps_c", [128, 1], F32)
    lnrow = P.al("lnrow", [1, 128], F32)
    lncol = P.al("lncol", [128, 1], F32)
    invf = P.al("invf", [128, 1], F32)
    st1 = P.al("st1", [128, 1], F32)
    st2 = P.al("st2", [128, 1], F32)
    st3 = P.al("st3", [128, 1], F32)
    stats_m = P.al("stats_m", [128, NTT], F32)
    bst = P.al("bst", [1, 4], F32)
    tbl_st = P.al("tbl_st", [128, 1], F32)
    stats_r = P.al("stats_r", [128, NTT], F32)
    mrow = P.al("mrow", [1, 128], F32)
    mcol = P.al("mcol", [128, 2], F32)
    stats_sb = P.al("stats_sb", [1, 16], F32)
    statg_sb = P.al("statg_sb", [1, 16], F32)
    wrow = P.al("wrow", [1, 8], F32)
    wrep = P.al("wrep", [128, 8], F32)
    selwT_sb = P.al("selwT_sb", [8, T], BF16)
    selw_row_off = P.o
    selw_row = P.al("selw_row", [1, T], BF16)
    bvr = P.al("bvr", [1, D], BF16)
    PERS = P.o
    K_ = 1024
    AR0 = (PERS + 31) // 32 * 32

    ATT = AT("ATT", [128, NDC, T], BF16, AR0 + 0)
    cos_bf = AT("cos_bf", [128, T], BF16, AR0 + 32 * K_)
    sin_bf = AT("sin_bf", [128, T], BF16, AR0 + 36 * K_)
    qp0 = AT("qp0", [128, T], BF16, AR0 + 40 * K_)
    qp1 = AT("qp1", [128, T], BF16, AR0 + 44 * K_)
    kp0 = AT("kp0", [128, T], BF16, AR0 + 48 * K_)
    kp1 = AT("kp1", [128, T], BF16, AR0 + 52 * K_)
    vp0 = AT("vp0", [128, NTT, 128], BF16, AR0 + 56 * K_)
    vp1 = AT("vp1", [128, NTT, 128], BF16, AR0 + 60 * K_)
    rraw = AT("rraw", [128, 512], BF16, AR0 + 64 * K_)
    rsw = AT("rsw", [128, 512], BF16, AR0 + 65 * K_)
    rA = AT("rA", [128, 512], BF16, AR0 + 66 * K_)
    rB = AT("rB", [128, 512], BF16, AR0 + 67 * K_)
    ptA = AT("ptA", [128, 2048], BF16, AR0 + 68 * K_)
    ptB = AT("ptB", [128, 2048], BF16, selw_row_off)
    wq_sl = [AT(f"wq_sl{i}", [128, NDC, 128], BF16, AR0 + (72 + 2 * i) * K_) for i in range(2)]
    wk_sl = [AT(f"wk_sl{i}", [128, NDC, 128], BF16, AR0 + (76 + 2 * i) * K_) for i in range(2)]
    wv_sl = [AT(f"wv_sl{i}", [128, NDC, 128], BF16, AR0 + (80 + 2 * i) * K_) for i in range(2)]
    vraw = AT("vraw", [128, T], BF16, AR0 + 84 * K_)
    sqbuf = AT("sqbuf", [128, T], BF16, AR0 + 84 * K_)
    recb = AT("recb", [128, 512], F32, AR0 + 88 * K_)
    rlb_sb = AT("rlb_sb", [128, 512], F32, AR0 + 90 * K_)
    ATTN_END = AR0 + 94 * K_

    tabw1 = AT("tabw1", [128, T], F32, AR0 + 0)
    tabw2 = AT("tabw2", [128, T], F32, AR0 + 8 * K_)
    tabw3i = AT("tabw3i", [128, T], I32, AR0 + 16 * K_)
    kbufs = (qp0.bitcast(F32), kp0.bitcast(F32))
    sinws = (vp0.bitcast(F32), vp1.bitcast(F32))

    wo_sl = AT("wo_sl", [128, NDC, D], BF16, AR0 + 40 * K_)
    bor = AT("bor", [1, D], BF16, AR0 + 56 * K_)

    lnsq1 = AT("lnsq1", [128, D], F32, AR0 + 68 * K_)
    lnsq2 = AT("lnsq2", [128, D], F32, AR0 + 58 * K_)
    ropeC = AT("ropeC", [128, 512], BF16, AR0 + 92 * K_)
    ropeD = AT("ropeD", [128, 512], BF16, AR0 + 93 * K_)
    xnt_bf1 = AT("xnt_bf1", [128, D], BF16, selw_row_off)
    xnt_bf2a = AT("xnt_bf2a", [128, D], BF16, AR0 + 62 * K_)
    xnt_bf2b = AT("xnt_bf2b", [128, D], BF16, AR0 + 92 * K_)
    xnt_f = AT("xnt_f", [128, D], F32, AR0 + 64 * K_)
    xf_sb = AT("xf_sb", [128, NDC, 128], F32, AR0 + 68 * K_)

    logits_sb = AT("logits_sb", [128, NTT, 8], F32, AR0 + 72 * K_)
    scores_sb = AT("scores_sb", [128, NTT, 8], F32, AR0 + 73 * K_)
    selbuf = AT("selbuf", [128, 128], F32, AR0 + 74 * K_)
    wcbuf = AT("wcbuf", [128, 128], F32, AR0 + 75 * K_)
    sc2buf = AT("sc2buf", [128, 128], F32, AR0 + 76 * K_)
    m1buf = AT("m1buf", [128, NTT], F32, AR0 + 77 * K_)
    m2buf = AT("m2buf", [128, NTT], F32, AR0 + 77 * K_ + 128)
    rdenb = AT("rdenb", [128, NTT], F32, AR0 + 77 * K_ + 256)
    selw_bf = AT("selw_bf", [128, 128], BF16, AR0 + 77 * K_ + 512)

    sg_sl = AT("sg_sl", [128, NDC, SI], BF16, AR0 + 0)
    su_sl = AT("su_sl", [128, NDC, SI], BF16, AR0 + 16 * K_)
    hs_sb = AT("hs_sb", [128, NDC, T], BF16, AR0 + 32 * K_)
    sd_sl = AT("sd_sl", [128, NDC, D], BF16, AR0 + 0)

    h_all = AT("h_all", [128, NE, 4, 1024], BF16, AR0 + 0)
    eg_sl = [AT(f"eg_sl{i}", [128, NDC, 256], BF16, AR0 + (64 + 4 * i) * K_) for i in range(2)]
    eu_sl = [AT(f"eu_sl{i}", [128, NDC, 256], BF16, AR0 + (72 + 4 * i) * K_) for i in range(2)]
    selwb = [AT(f"selwb{i}", [128, T // 2], BF16, AR0 + 80 * K_ + 2 * K_ * i) for i in range(2)]
    sil_bf = AT("sil_bf", [128, 512], BF16, AR0 + 84 * K_)
    silw_bf = AT("silw_bf", [128, 512], BF16, AR0 + 85 * K_)
    sil4_bf = AT("sil4_bf", [128, 2048], BF16, AR0 + 84 * K_)
    ed_sl = [[AT(f"ed_sl{p}_{e}", [128, 4, 256], BF16, AR0 + (64 + 16 * p + 2 * e) * K_)
              for e in range(NE)] for p in range(2)]

    psA = nc.alloc_psum_tensor("psA", [128, 4096], F32)
    psA_bf = psA.bitcast(BF16)
    def PB(bank, banks=1):
        return psA[:, 512 * bank: 512 * (bank + banks)]
    def PBbf(bank, banks=1):
        return psA_bf[:, 1024 * bank: 1024 * (bank + banks)]

    DBG.clear()
    DBG.update(dict(x_tm=x_tm, xn_fm=xn_fm, cos_bf=cos_bf, sin_bf=sin_bf,
                    qp0=qp0, qp1=qp1, kp0=kp0, kp1=kp1, vp0=vp0, vp1=vp1,
                    ATT=ATT, logits_sb=logits_sb, scores_sb=scores_sb,
                    selbuf=selbuf, wrow=wrow, wrep=wrep, selwT_sb=selwT_sb,
                    m1buf=m1buf, m2buf=m2buf, h_all=h_all, hs_sb=hs_sb,
                    stats_sb=stats_sb, statg_sb=statg_sb, invf=invf, mcol=mcol))

    em = Emitter(nc)

    def walk():
        opn = em.op
        dma = em.dma
        war = {}

        tok_xc = []
        for xc, xst in enumerate(("egx0", "egx1", "edx0", "edx1")):
            tok_xc.append(dma("dg", [], lambda e, xc=xc: e.dma_start(
                out=x_tm[:, 4 * xc:4 * (xc + 1), :],
                in_=x_in.rearrange("(tt p) d -> p tt d", p=128)[:, 4 * xc:4 * (xc + 1), :]),
                stream=xst))
        cl = {}
        for nm, dst, src in [
                ("idb", idn_bf, idn_bf_c), ("idf", idn_f, idn_f_c), ("perm", perm_s, perm_c),
                ("tri", tri_s, tri_c), ("o1b", ones1b, ones1b_c), ("o1f", ones1f, ones1f_c),
                ("ocb", onescb, onescb_c), ("ocf", onescf, onescf_c), ("obl", onesblk, onesblk_c),
                ("sgn", signc, signc_c), ("jc", jcol, jcol_c), ("c10", c10k, c10k_c),
                ("bqc", bqc, din["bqc"]), ("bkc", bkc, din["bkc"]),
                ("bvc", bvc, din["bvc"]), ("lb", lb_sb, din["lb"])]:
            cl[nm] = dma("dy", [], lambda e, d=dst, s=src: e.dma_start(out=d[...], in_=s), stream="pro")
        cl["gw"] = dma("dy", [], lambda e: e.dma_start(
            out=gw_sl[...], in_=gwT.rearrange("(dc p) e -> p dc e", p=128)), stream="pro")
        tok_tb = dma("dy", [], lambda e: e.dma_start(out=tabw1[:, :], in_=tbase_c), stream="pro")
        tok_bv = dma("dg", [], lambda e: e.dma_start(out=bvr[:, :], in_=din["bvr"]), stream="prog")

        slab_toks = {}
        def load_qkv_slabs(c):
            par = c % 2
            deps = [war.get(("qkvslab", par))]
            tq = dma("dg", deps, lambda e, c=c, par=par: e.dma_start(
                out=wq_sl[par][...],
                in_=wqT.rearrange("(dc p) e -> p dc e", p=128)[:, :, 128 * c:128 * (c + 1)]),
                stream=f"qkv{par}")
            tk = dma("dg", deps, lambda e, c=c, par=par: e.dma_start(
                out=wk_sl[par][...],
                in_=wkT.rearrange("(dc p) e -> p dc e", p=128)[:, :, 128 * c:128 * (c + 1)]),
                stream=f"qkv{par}")
            tv = dma("dg", deps, lambda e, c=c, par=par: e.dma_start(
                out=wv_sl[par][...],
                in_=wvT.rearrange("(dc p) e -> p dc e", p=128)[:, :, 128 * c:128 * (c + 1)]),
                stream=f"qkv{par}")
            slab_toks[c] = (tv, tv, tv)
        load_qkv_slabs(0)

        tok_pro = (em.stream("pro").sem, em.stream("pro").n)
        tok_x = tok_tb = tok_pro
        for _k in list(cl):
            cl[_k] = tok_pro
        tok_eps = opn("v", [], lambda e: e.memset(eps_c[:, :], 1e-5))
        tok_ps0 = opn("v", [], lambda e: e.memset(psA[:, 0:2048], 0.0))
        war["lnps0"] = tok_ps0
        war["lnps1"] = tok_ps0
        t_lbmm = opn("t", [cl["o1f"], cl["lb"], war.get("pb7")], lambda e: e.matmul(
            PB(7)[:, 0:8], ones1f[:, :], lb_sb[:, :], start=True, stop=True))
        tok_lbrep = opn("v", [t_lbmm], lambda e: e.tensor_copy(lb_rep[:, :], PB(7)[:, 0:8]))
        war["pb7"] = tok_lbrep


        tbs = {}
        if "tables" in PHASES:
            def build_table(dst_bf, shift, sign_fold, deps):
                tu = opn("v", deps, lambda e: e.tensor_scalar(
                    tabw2[:, :], tabw1[:, :], shift, None, ALU.add))
                last = []
                for half in range(2):
                    sl = slice(1024 * half, 1024 * (half + 1))
                    kb = kbufs[half]; sw = sinws[half]
                    opn("v", [tu], lambda e, sl=sl, sw=sw: e.tensor_scalar(
                        sw[:, :], tabw2[:, sl], INV_C1, 0.5, ALU.mult, ALU.add))
                    opn("v", [], lambda e, sl=sl, sw=sw: e.tensor_copy(tabw3i[:, sl], sw[:, :]))
                    opn("v", [], lambda e, sl=sl, kb=kb: e.tensor_copy(kb[:, :], tabw3i[:, sl]))
                    opn("v", [], lambda e, sw=sw, kb=kb: e.tensor_tensor(
                        sw[:, :], kb[:, :], sw[:, :], ALU.is_gt))
                    opn("v", [], lambda e, sw=sw, kb=kb: e.tensor_tensor(
                        kb[:, :], kb[:, :], sw[:, :], ALU.subtract))
                    opn("v", [], lambda e, sl=sl, kb=kb: e.scalar_tensor_tensor(
                        tabw2[:, sl], kb[:, :], -C1, tabw2[:, sl], ALU.mult, ALU.add))
                    opn("v", [], lambda e, sl=sl, kb=kb: e.scalar_tensor_tensor(
                        tabw2[:, sl], kb[:, :], C2P, tabw2[:, sl], ALU.mult, ALU.add))
                    opn("v", [], lambda e, sl=sl: e.tensor_scalar(
                        tabw2[:, sl], tabw2[:, sl], PI_CLAMP, None, ALU.min))
                    c2t = opn("v", [], lambda e, sl=sl: e.tensor_scalar(
                        tabw2[:, sl], tabw2[:, sl], -PI_CLAMP, None, ALU.max))
                    tsn = opn("s", [c2t], lambda e, sl=sl, sw=sw: e.activation(
                        sw[:, :], tabw2[:, sl], AF.Sin))
                    if sign_fold:
                        tc = opn("v", [tsn, cl["sgn"]], lambda e, sl=sl, sw=sw: e.tensor_scalar_mul(
                            dst_bf[:, sl], sw[:, :], signc[:, :]))
                    else:
                        tc = opn("v", [tsn], lambda e, sl=sl, sw=sw: e.tensor_copy(
                            dst_bf[:, sl], sw[:, :]))
                    last.append(tc)
                    yield
                return last

            def tables_gen():
                t_ln = opn("s", [cl["c10"]], lambda e: e.activation(lnrow[:, 0:1], c10k[:, :], AF.Ln))
                t_lnr = opn("v", [t_ln], lambda e: e.tensor_copy(
                    lnrow[:, :], lnrow[0:1, 0:1].broadcast_to([1, 128])))
                t_lnc = dma("dy", [t_lnr], lambda e: e.dma_start(out=lncol[:, :], in_=lnrow[:, :]),
                            stream="lnc")
                t_ec = opn("v", [t_lnc, cl["jc"]], lambda e: e.tensor_tensor(
                    tbl_st[:, :], jcol[:, :], lncol[:, :], ALU.mult))
                t_iv = opn("s", [t_ec], lambda e: e.activation(
                    invf[:, :], tbl_st[:, :], AF.Exp, scale=float(-1.0 / 32.0)))
                t_ang = opn("v", [t_iv, tok_tb], lambda e: e.tensor_scalar_mul(
                    tabw1[:, :], tabw1[:, :], invf[:, :]))
                yield
                ctoks = yield from build_table(cos_bf, SHIFT_COS, False, [t_ang])
                stoks = yield from build_table(sin_bf, SHIFT_SIN, True, [t_ang] + ctoks)
                tbs["tok"] = stoks[-1]
            tbgen = tables_gen()
        else:
            tbgen = None

        def drive_tb(g):
            if g is None:
                return None
            try:
                next(g)
                return g
            except StopIteration:
                return None


        def ln_tile(tt, second, lst, tdep=None):
            lnsq = lnsq2 if second else lnsq1
            xnt_bf = ((xnt_bf2a, xnt_bf2b)[tt % 2]) if second else xnt_bf1
            xt = x_tm[:, tt, :]
            tsq = opn("s", [war.get("lnsq_r"), tok_xc[tt // 4], tdep],
                      lambda e, xt=xt: e.activation(
                lnsq[:, :], xt, AF.Square))
            tss = opn("v", [tsq], lambda e: e.reduce_sum(st1[:, :], lnsq[:, :], axis=AX.X))
            war["lnsq_r"] = tss
            tms = opn("v", [tok_xc[tt // 4], tdep], lambda e, xt=xt: e.reduce_sum(st2[:, :], xt, axis=AX.X))
            opn("v", [], lambda e: e.tensor_scalar_mul(st2[:, :], st2[:, :], float(1.0 / D)))
            opn("v", [], lambda e: e.tensor_tensor(st3[:, :], st2[:, :], st2[:, :], ALU.mult))
            tva = opn("v", [], lambda e: e.scalar_tensor_tensor(
                st3[:, :], st1[:, :], float(1.0 / D), st3[:, :], ALU.mult, ALU.subtract))
            tsd = opn("s", [tva, tok_eps], lambda e: e.activation(
                st3[:, :], st3[:, :], AF.Sqrt, bias=eps_c[:, :]))
            trs = opn("v", [tsd], lambda e: e.reciprocal(st3[:, :], st3[:, :]))
            tap = opn("v", [war.get("xnt_r")], lambda e, xt=xt: e.tensor_scalar(
                xnt_bf[:, :], xt, st2[:, :], st3[:, :], ALU.subtract, ALU.mult))
            if second:
                opn("v", [war.get("stm_r")], lambda e, tt=tt: e.tensor_copy(
                    stats_m[:, tt:tt + 1], st2[:, :]))
                opn("v", [war.get("str_r")], lambda e, tt=tt: e.tensor_copy(
                    stats_r[:, tt:tt + 1], st3[:, :]))

            def part2():
                par = tt % 2
                pb = PBbf(5 + par) if second else PBbf(par)
                ttr = None
                for dc in range(NDC):
                    ttr = opn("t", [tap, lst["evac_war"][par], cl["idb"]] if dc == 0 else [],
                              lambda e, dc=dc, pb=pb: e.transpose(
                        pb[:, 128 * dc:128 * (dc + 1)], xnt_bf[:, 128 * dc:128 * (dc + 1)],
                        idn_bf[:, :]))
                war["xnt_r"] = ttr
                tev = opn("v", [ttr], lambda e, tt=tt, pb=pb: e.tensor_copy(
                    xn_fm[:, :, 128 * tt:128 * (tt + 1)],
                    pb.rearrange("p (dc q) -> p dc q", q=128)))
                lst["evac_war"][par] = tev
                return tev
            return part2

        def ln_state():
            return {"evac_war": {0: war.get("lnps0"), 1: war.get("lnps1")}}

        def ln_save(lst):
            war["lnps0"] = lst["evac_war"][0]
            war["lnps1"] = lst["evac_war"][1]

        def layernorm_pass(second, tile_deps=None):
            lst = ln_state()
            tile_toks = []
            for tt in range(NTT):
                tdep = tile_deps[tt] if tile_deps is not None else None
                tile_toks.append(ln_tile(tt, second, lst, tdep)())
            ln_save(lst)
            return tile_toks

        ln1_tiles = None
        if "ln1" in PHASES:
            lst1 = ln_state()
            ln1_tiles = []
            for tt in range(NTT):
                ln1_tiles.append(ln_tile(tt, False, lst1)())
                if tt % 3 == 2:
                    tbgen = drive_tb(tbgen)
            ln_save(lst1)
            war["pb0123"] = ln1_tiles[-1]
        while tbgen is not None:
            tbgen = drive_tb(tbgen)
        tok_tables = tbs.get("tok")

        if "attn" in PHASES:
            pair_state = {}

            def qkv_gen(c):
                par = c % 2
                wq = wq_sl[par]; wk = wk_sl[par]; wv = wv_sl[par]
                qbuf = (qp0, qp1)[par]; kbuf = (kp0, kp1)[par]; vbuf = (vp0, vp1)[par]
                tq_sl, tk_sl, tv_sl = slab_toks[c]
                qk_done = []
                for which, wsl, wtok, dstbuf, bias in (
                        ("q", wq, tq_sl, qbuf, bqc), ("k", wk, tk_sl, kbuf, bkc)):
                    for tg in range(4):
                        sl = slice(512 * tg, 512 * (tg + 1))
                        rr = (rraw, rsw)[tg % 2]
                        rs = (rA, rB)[tg % 2]
                        mm = None
                        deps0 = [wtok, war.get("pb7"), cl["bqc"],
                                 ln1_tiles[4 * tg + 3] if ln1_tiles else None]
                        for dc in range(NDC):
                            mm = opn("t", deps0 if dc == 0 else [],
                                     lambda e, dc=dc, wsl=wsl, sl=sl: e.matmul(
                                PB(7), wsl[:, dc, :], xn_fm[:, dc, sl],
                                start=(dc == 0), stop=(dc == NDC - 1)))
                        tev = opn("s", [mm, war.get(("rr_r1", tg % 2)), war.get(("rr_r2", tg % 2))],
                                  lambda e, bias=bias, c=c, rr=rr: e.activation(
                            rr[:, :], PB(7), AF.Identity, bias=bias[:, c:c + 1]))
                        yield
                        msw = opn("t", [tev, cl["perm"]], lambda e, rr=rr: e.matmul(
                            PB(7), perm_s[:, :], rr[:, :], start=True, stop=True))
                        war["pb7"] = msw
                        tsw = opn("s", [msw, war.get(("rs_r", tg % 2))],
                                  lambda e, rs=rs: e.activation(rs[:, :], PB(7), AF.Copy))
                        war["pb7"] = tsw
                        ta = opn("v", [tev], lambda e, sl=sl, rr=rr: e.tensor_tensor(
                            ropeC[:, :], rr[:, :], cos_bf[:, sl], ALU.mult))
                        war[("rr_r1", tg % 2)] = ta
                        war[("rr_r2", tg % 2)] = msw
                        tb = opn("g", [tsw, war.get("ropeD_r"), tok_tables],
                                 lambda e, sl=sl, rs=rs: e.tensor_tensor(
                            ropeD[:, :], rs[:, :], sin_bf[:, sl], ALU.mult))
                        war[("rs_r", tg % 2)] = tb
                        tc = opn("v", [ta, tb], lambda e, dstbuf=dstbuf, sl=sl: e.tensor_tensor(
                            dstbuf[:, sl], ropeC[:, :], ropeD[:, :], ALU.add))
                        war["ropeD_r"] = tc
                        qk_done.append(tc)
                        yield
                v_done = []
                tevs = {}
                for tg in range(4):
                    sl = slice(512 * tg, 512 * (tg + 1))
                    mm = None
                    deps0 = [tv_sl, war.get("pb7"), qk_done[-1]]
                    for dc in range(NDC):
                        mm = opn("t", deps0 if dc == 0 else [],
                                 lambda e, dc=dc, sl=sl, wv=wv: e.matmul(
                            PB(7), wv[:, dc, :], xn_fm[:, dc, sl],
                            start=(dc == 0), stop=(dc == NDC - 1)))
                    tevs[tg] = opn("s", [mm, war.get("vraw_r")], lambda e, sl=sl, c=c: e.activation(
                        vraw[:, sl], PB(7), AF.Identity, bias=bvc[:, c:c + 1]))
                    war["pb7"] = tevs[tg]
                    yield
                for half in range(2):
                    ttr = None
                    for j in range(NDC):
                        dep0 = [tevs[2 * half], tevs[2 * half + 1], war.get("pb7")] \
                            if j == 0 else []
                        ttr = opn("t", dep0, lambda e, half=half, j=j: e.transpose(
                            PBbf(7)[:, 128 * j:128 * (j + 1)],
                            vraw[:, 1024 * half + 128 * j:1024 * half + 128 * (j + 1)],
                            idn_bf[:, :]))
                    evv = opn("v", [ttr], lambda e, half=half, vbuf=vbuf: e.tensor_copy(
                        vbuf[:, 8 * half:8 * (half + 1), :],
                        PBbf(7).rearrange("p (a b) -> p a b", b=128)))
                    war["pb7"] = evv
                    for _ in range(8):
                        v_done.append(evv)
                    yield
                nrm = []
                for nm, buf, qtok in (("q", qbuf, qk_done[3]), ("k", kbuf, qk_done[7])):
                    tqs = opn("s", [qtok, war.get("sq_r"), evv],
                              lambda e, buf=buf: e.activation(sqbuf[:, :], buf[:, :], AF.Square))
                    dst = bst[0:1, 0:1] if nm == "q" else bst[0:1, 1:2]
                    run = None
                    for tg in range(4):
                        mmn = opn("t", [tqs, war.get("pb7")],
                                  lambda e, tg=tg: e.matmul(
                            PB(7)[0:1, :], onescb[:, :],
                            sqbuf[:, 512 * tg:512 * (tg + 1)], start=True, stop=True))
                        tm1 = opn("v", [mmn, run], lambda e: e.reduce_max(
                            bst[0:1, 2:3], PB(7)[0:1, :], axis=AX.X))
                        war["pb7"] = tm1
                        if tg == 0:
                            run = opn("v", [tm1], lambda e, dst=dst: e.tensor_copy(
                                dst, bst[0:1, 2:3]))
                        else:
                            run = opn("v", [tm1], lambda e, dst=dst: e.tensor_tensor(
                                dst, dst, bst[0:1, 2:3], ALU.max))
                    war["sq_r"] = run
                    nrm.append(run)
                    yield
                tpd = opn("v", [nrm[1]], lambda e: e.tensor_tensor(
                    bst[0:1, 0:1], bst[0:1, 0:1], bst[0:1, 1:2], ALU.add))
                tmr = opn("v", [tpd, war.get("mrow_r")], lambda e: e.tensor_scalar(
                    mrow[:, :], bst[0:1, 0:1].broadcast_to([1, 128]), float(-1.0 / 16.0),
                    None, ALU.mult))
                tmc = dma("dy", [tmr, war.get(("mcol_r", par))], lambda e, par=par: e.dma_start(
                    out=mcol[:, par:par + 1], in_=mrow[:, :]), stream="mcl")
                war["mrow_r"] = tmc
                pair_state[c] = dict(qbuf=qbuf, kbuf=kbuf, vbuf=vbuf, qk_done=qk_done,
                                     v_done=v_done, tmc=tmc, par=par)

            def drive(gen):
                if gen is None:
                    return None
                try:
                    next(gen)
                    return gen
                except StopIteration:
                    return None

            gen = qkv_gen(0)
            while drive(gen):
                pass
            next_gen = None

            for c in range(NP):
                par = c % 2
                st_ = pair_state[c]
                qbuf = st_["qbuf"]; kbuf = st_["kbuf"]; vbuf = st_["vbuf"]
                qk_done = st_["qk_done"]; v_done = st_["v_done"]; tmc = st_["tmc"]
                if c + 1 < NP:
                    load_qkv_slabs(c + 1)
                    next_gen = qkv_gen(c + 1)

                for qg in range(4):
                    ktmax = 4 * qg + 4
                    ngrp = ktmax // 2
                    exp_war = {0: war.get("ptavA"), 1: war.get("ptavB")}
                    sb_war = [war.get("pb0123"), war.get("pb2345"),
                              war.get("pb0"), war.get("pb1"), war.get("pb2"), war.get("pb3")]
                    tok_av = None
                    tok_deno = None
                    last_exp = None
                    pend = {}

                    def emit_scores_group(g):
                        nonlocal last_exp, sb_war
                        ptb = (ptA, ptB)[g % 2]
                        info = []
                        first = True
                        for kt in (2 * g, 2 * g + 1):
                            spar = kt - 2 * g
                            sps = PB(0, 2) if spar == 0 else PB(2, 2)
                            d = kt - 4 * qg
                            col0 = 128 * d if d >= 0 else 0
                            qlo = 512 * qg + col0
                            qn = 512 - col0
                            dep_s = ([qk_done[-1], exp_war[g % 2]] + sb_war) if first else []
                            first = False
                            opn("t", dep_s, lambda e, kt=kt, qlo=qlo, qn=qn, col0=col0,
                                sps=sps: e.matmul(
                                sps[:, col0:col0 + qn], kbuf[0:64, 128 * kt:128 * (kt + 1)],
                                qbuf[0:64, qlo:qlo + qn], start=True, stop=True))
                            ms2 = opn("t", [], lambda e, kt=kt, qlo=qlo, qn=qn, col0=col0,
                                      sps=sps: e.matmul(
                                sps[:, 512 + col0:512 + col0 + qn],
                                kbuf[64:128, 128 * kt:128 * (kt + 1)],
                                qbuf[64:128, qlo:qlo + qn], start=True, stop=True))
                            info.append((kt, spar, col0, qn, ms2))
                        e1 = opn("s", [info[-1][-1], tmc], lambda e, ptb=ptb, par=par: e.activation(
                            ptb[:, 0:2048], psA[:, 0:2048], AF.Exp,
                            bias=mcol[:, par:par + 1], scale=0.125))
                        pt_ready = [e1]
                        last_exp = e1
                        for kt, spar, col0, qn, _ in info:
                            if kt - 4 * qg >= 0:
                                pcol = 1024 * spar
                                m1_ = opn("v", [e1, cl["tri"]], lambda e, col0=col0,
                                          ptb=ptb, pcol=pcol: e.tensor_tensor(
                                    ptb[:, pcol + col0:pcol + col0 + 128],
                                    ptb[:, pcol + col0:pcol + col0 + 128],
                                    tri_s[:, :], ALU.mult))
                                m2_ = opn("v", [m1_], lambda e, col0=col0, ptb=ptb,
                                          pcol=pcol: e.tensor_tensor(
                                    ptb[:, pcol + 512 + col0:pcol + 512 + col0 + 128],
                                    ptb[:, pcol + 512 + col0:pcol + 512 + col0 + 128],
                                    tri_s[:, :], ALU.mult))
                                pt_ready += [m1_, m2_]
                                last_exp = m2_ if False else last_exp
                        sb_war = [last_exp]
                        pend[g] = (pt_ready, info, ptb)

                    def emit_avdeno_group(g, av_start, av_stop):
                        nonlocal tok_av, tok_deno
                        pt_ready, info, ptb = pend.pop(g)
                        first = True
                        for kt, spar, col0, qn, _ in info:
                            pcol = 1024 * spar
                            st = av_start and first
                            sp = av_stop and (kt == info[-1][0])
                            av_dep = (pt_ready + [v_done[info[-1][0]],
                                      war.get("pb4") if st else None,
                                      war.get("pb5") if st else None]) if first else []
                            first = False
                            opn("t", av_dep, lambda e, kt=kt, col0=col0, qn=qn, ptb=ptb,
                                pcol=pcol, st=st, sp=sp: e.matmul(
                                PB(4)[0:64, col0:col0 + qn], vbuf[:, kt, 0:64],
                                ptb[:, pcol + col0:pcol + col0 + qn], start=st, stop=sp,
                                tile_position=(0, 0)))
                            opn("t", [], lambda e, kt=kt, col0=col0, qn=qn, ptb=ptb,
                                pcol=pcol, st=st, sp=sp: e.matmul(
                                PB(4)[64:128, col0:col0 + qn], vbuf[:, kt, 64:128],
                                ptb[:, pcol + 512 + col0:pcol + 512 + col0 + qn],
                                start=st, stop=sp, tile_position=(0, 64)))
                            opn("t", [], lambda e, col0=col0, qn=qn, ptb=ptb, pcol=pcol,
                                st=st, sp=sp: e.matmul(
                                PB(5)[0:1, col0:col0 + qn], onescb[:, :],
                                ptb[:, pcol + col0:pcol + col0 + qn], start=st, stop=sp,
                                tile_position=(0, 0)))
                            d2 = opn("t", [], lambda e, col0=col0, qn=qn, ptb=ptb,
                                     pcol=pcol, st=st, sp=sp: e.matmul(
                                PB(5)[32:33, col0:col0 + qn], onescb[:, :],
                                ptb[:, pcol + 512 + col0:pcol + 512 + col0 + qn],
                                start=st, stop=sp, tile_position=(0, 32)))
                            tok_av = d2
                            tok_deno = d2
                        exp_war[g % 2] = tok_deno

                    for g in range(ngrp):
                        emit_scores_group(g)
                        next_gen = drive(next_gen)
                        if g == 0 and war.get("pending_epi") is not None:
                            war.pop("pending_epi")()
                        if g > 0:
                            emit_avdeno_group(g - 1, av_start=(g == 1), av_stop=False)
                            next_gen = drive(next_gen)
                    emit_avdeno_group(ngrp - 1, av_start=(ngrp == 1), av_stop=True)
                    war["ptavA"] = exp_war[0]
                    war["ptavB"] = exp_war[1]
                    war["pb0123"] = sb_war[0]
                    war[("mcol_r", par)] = last_exp
                    last_exp_of_pair = last_exp

                    def make_epilogue(c=c, qg=qg, tok_av=tok_av, tok_deno=tok_deno):
                        def epi():
                            r1 = opn("s", [tok_deno, war.get("recb_r")], lambda e: e.activation(
                                recb[0:1, :], PB(5)[0:1, :], AF.Copy))
                            r2 = opn("s", [r1], lambda e: e.activation(
                                recb[32:33, :], PB(5)[32:33, :], AF.Copy))
                            war["pb5"] = r2
                            b1 = opn("t", [r1, r2, war.get("pb6"), cl["obl"]], lambda e: e.matmul(
                                PB(6)[0:64, :], onesblk[0:1, :], recb[0:1, :], start=True, stop=True,
                                tile_position=(0, 0)))
                            b2 = opn("t", [], lambda e: e.matmul(
                                PB(6)[64:128, :], onesblk[32:33, :], recb[32:33, :], start=True,
                                stop=True, tile_position=(32, 64)))
                            war["recb_r"] = b2
                            rv = opn("v", [b2], lambda e: e.reciprocal(rlb_sb[:, :], PB(6)))
                            war["pb6"] = rv
                            ev = opn("v", [rv, tok_av], lambda e: e.tensor_tensor(
                                ATT[:, c, 512 * qg:512 * (qg + 1)], PB(4), rlb_sb[:, :], ALU.mult))
                            war["pb4"] = ev
                        return epi
                    war["pending_epi"] = make_epilogue()
                while drive(next_gen):
                    pass
                next_gen = None
        if war.get("pending_epi") is not None:
            war.pop("pending_epi")()
        em.gate()

        if "wo" in PHASES:
            dma("dg", [], lambda e: e.dma_start(
                out=wo_sl[...], in_=woT.rearrange("(dc p) e -> p dc e", p=128)), stream="wox")
            tok_bo = dma("dg", [], lambda e: e.dma_start(out=bor[:, :], in_=din["bor"]),
                         stream="wox")
            tok_wo = tok_bo
            prev_ev = {0: None, 1: None}
            wo_last_mm = None
            ln2_tiles = [] if "ln2" in PHASES else None
            lnst = ln_state()
            pend_p2 = None
            for tt in range(NTT):
                mm = {}
                for half in range(2):
                    m = None
                    for dc in range(NDC):
                        m = opn("t", [tok_wo, tok_bo, prev_ev[half]] if dc == 0 else [],
                                lambda e, dc=dc, tt=tt, half=half: e.matmul(
                            PB(half), ATT[:, dc, 128 * tt:128 * (tt + 1)],
                            wo_sl[:, dc, 512 * half:512 * (half + 1)],
                            start=(dc == 0), stop=False))
                    m = opn("t", [], lambda e, half=half: e.matmul(
                        PB(half), ones1b[:, :], bor[0:1, 512 * half:512 * (half + 1)],
                        start=False, stop=True))
                    mm[half] = m
                    wo_last_mm = m
                for half in range(2):
                    tev = opn("v", [mm[half]], lambda e, tt=tt, half=half: e.tensor_tensor(
                        x_tm[:, tt, 512 * half:512 * (half + 1)], PB(half),
                        x_tm[:, tt, 512 * half:512 * (half + 1)], ALU.add))
                    prev_ev[half] = tev
                if ln2_tiles is not None:
                    p2 = ln_tile(tt, True, lnst, tdep=prev_ev[1])
                    if pend_p2 is not None:
                        ln2_tiles.append(pend_p2())
                    pend_p2 = p2
            if ln2_tiles is not None and pend_p2 is not None:
                ln2_tiles.append(pend_p2())
            ln_save(lnst)
            war["pb0"] = prev_ev[0]
            war["pb1"] = prev_ev[1]
            war["wo_last_mm"] = wo_last_mm
        else:
            ln2_tiles = None
            if "ln2" in PHASES:
                ln2_tiles = layernorm_pass(second=True)
        em.gate()

        gst = {"t_ste": None}
        GS = [0]
        def gs():
            GS[0] += 1
            return GS[0] <= GATE_STOP

        gp = {"tapf": {}, "tfr": {}, "tfe": {}, "tgm": {}, "tge": {}}
        def gate_stage(k):
            if "ln2" not in PHASES:
                return
            j = k - 2
            if 0 <= j < NTT:
                tgm = None
                for dc in range(NDC):
                    tgm = opn("t", [gp["tfe"][j], cl["gw"], gp["tge"].get(j - 1)] if dc == 0 else [],
                              lambda e, dc=dc: e.matmul(
                        PB(6)[:, 0:8], xf_sb[:, dc, :], gw_sl[:, dc, :],
                        start=(dc == 0), stop=(dc == NDC - 1)))
                gp["tgm"][j] = tgm
                gp["tge"][j] = opn("v", [tgm], lambda e, j=j: e.tensor_copy(
                    logits_sb[:, j, :], PB(6)[:, 0:8]))
                if j == NTT - 1:
                    war["lngate"] = gp["tge"][j]
            j = k - 1
            if 0 <= j < NTT:
                tfr = None
                for dc in range(NDC):
                    tfr = opn("t", [gp["tapf"][j], cl["idf"], gp["tfe"].get(j - 1)] if dc == 0 else [],
                              lambda e, dc=dc: e.transpose(
                        PB(4, 2)[:, 128 * dc:128 * (dc + 1)],
                        xnt_f[:, 128 * dc:128 * (dc + 1)], idn_f[:, :]))
                gp["tfr"][j] = tfr
                gp["tfe"][j] = opn("v", [tfr, gp["tgm"].get(j - 1)], lambda e: e.tensor_copy(
                    xf_sb[:, :, :], PB(4, 2).rearrange("p (dc q) -> p dc q", q=128)))
            if k < NTT:
                j = k
                gp["tapf"][j] = opn("v", [gp["tfr"].get(j - 1)], lambda e, j=j: e.tensor_scalar(
                    xnt_f[:, :], x_tm[:, j, :], stats_m[:, j:j + 1], stats_r[:, j:j + 1],
                    ALU.subtract, ALU.mult))

        def emit_gating_a1():
            if "gate" not in PHASES:
                return
            sc3 = scores_sb.rearrange("p a b -> p (a b)")
            lg3 = logits_sb.rearrange("p a b -> p (a b)")
            t_lb = gs() and opn("v", [tok_lbrep, war.get("lngate")], lambda e: e.tensor_tensor(
                logits_sb[:, :, :], logits_sb[:, :, :],
                lb_rep[:, None, :].broadcast_to([128, NTT, 8]), ALU.add))
            t_sg = gs() and opn("s", [t_lb], lambda e: e.activation(sc3[:, :], lg3[:, :], AF.Sigmoid))
            t_m1 = gs() and opn("v", [t_sg], lambda e: e.reduce_max(m1buf[:, :], scores_sb[:, :, :], axis=AX.X))
            t_t1 = gs() and opn("v", [t_m1], lambda e: e.tensor_tensor(
                selbuf.rearrange("p (a b) -> p a b", b=8), scores_sb[:, :, :],
                m1buf[:, :, None].broadcast_to([128, NTT, 8]), ALU.is_ge))
            t_s2 = gs() and opn("v", [t_t1], lambda e: e.scalar_tensor_tensor(
                sc2buf[:, :], selbuf[:, :], -1e30, sc3[:, :], ALU.mult, ALU.add))
            t_m2 = gs() and opn("v", [t_s2], lambda e: e.reduce_max(
                m2buf[:, :], sc2buf.rearrange("p (a b) -> p a b", b=8), axis=AX.X))
            t_sel = gs() and opn("v", [t_m2], lambda e: e.tensor_tensor(
                selbuf.rearrange("p (a b) -> p a b", b=8), scores_sb[:, :, :],
                m2buf[:, :, None].broadcast_to([128, NTT, 8]), ALU.is_ge))
            gs() and opn("v", [t_m2], lambda e: e.tensor_tensor(rdenb[:, :], m1buf[:, :], m2buf[:, :], ALU.add))
            gs() and opn("v", [], lambda e: e.tensor_scalar(rdenb[:, :], rdenb[:, :], 1e-8, None, ALU.add))
            t_rd = gs() and opn("v", [], lambda e: e.reciprocal(rdenb[:, :], rdenb[:, :]))
            t_wc = gs() and opn("v", [t_sel, t_rd], lambda e: e.tensor_tensor(
                wcbuf[:, :], selbuf[:, :], sc3[:, :], ALU.mult))
            t_wc2 = gs() and opn("v", [t_wc], lambda e: e.tensor_tensor(
                wcbuf.rearrange("p (a b) -> p a b", b=8),
                wcbuf.rearrange("p (a b) -> p a b", b=8),
                rdenb[:, :, None].broadcast_to([128, NTT, 8]), ALU.mult))
            gst["t_wc2"] = t_wc2
            gst["t_sel"] = t_sel

        def emit_gating_a2():
            if "gate" not in PHASES:
                return
            t_wc2 = gst.get("t_wc2")
            t_sel = gst.get("t_sel")
            t_sS = gs() and opn("t", [t_wc2, war.get("pb7"), cl["ocf"]], lambda e: e.matmul(
                PB(7)[0:1, 0:128], onescf[:, :], wcbuf[:, :], start=True, stop=True))
            t_sC = gs() and opn("t", [t_sel], lambda e: e.matmul(
                PB(7)[0:1, 128:256], onescf[:, :], selbuf[:, :], start=True, stop=True))
            t_cp = gs() and opn("v", [t_sS, t_sC], lambda e: e.tensor_copy(
                rlb_sb[0:1, 0:256], PB(7)[0:1, 0:256]))
            war["pb7"] = t_cp
            t_tr = t_cp
            for w in (64, 32, 16, 8):
                t_tr = gs() and opn("v", [t_tr], lambda e, w=w: e.tensor_tensor(
                    rlb_sb[0:1, 0:w], rlb_sb[0:1, 0:w], rlb_sb[0:1, w:2 * w], ALU.add))
                t_tr = gs() and opn("v", [t_tr], lambda e, w=w: e.tensor_tensor(
                    rlb_sb[0:1, 128:128 + w], rlb_sb[0:1, 128:128 + w],
                    rlb_sb[0:1, 128 + w:128 + 2 * w], ALU.add))
            t_db = gs() and dma("dy", [t_tr], lambda e: e.dma_start(
                out=ar_in,
                in_=rlb_sb[0:1, 0:256].rearrange("o (a b) -> o a b", b=128)[:, :, 0:8]),
                stream="ari")
            ccch = em.stream("ccx")
            do_cc = gs()
            if do_cc and em.cur == "g":
                em.wait(t_db)
                e = em.eng
                e.collective_compute(
                    "AllReduce", ALU.add, replica_groups=[list(range(AR_WORLD))],
                    ins=[ar_in.opt()], outs=[ar_out.opt()]).then_inc(ccch.sem, 1)
            if do_cc:
                ccch.n += 1
            t_cc = (ccch.sem, ccch.n) if do_cc else None
            gst["t_cc"] = t_cc

        def emit_gating_b():
            if "gate" not in PHASES:
                return
            t_cc = gst.get("t_cc")
            t_sel = gst.get("t_sel")
            t_gb = gs() and dma("dy", [t_cc], lambda e: e.dma_start(out=statg_sb[:, :], in_=ar_out),
                       stream="aro")
            t_cm = gs() and opn("v", [t_gb], lambda e: e.tensor_scalar_max(
                mrow[0:1, 0:8], statg_sb[0:1, 8:16], 1.0))
            t_rc = gs() and opn("v", [t_cm], lambda e: e.reciprocal(mrow[0:1, 0:8], mrow[0:1, 0:8]))
            t_wm = gs() and opn("v", [t_rc], lambda e: e.tensor_tensor(
                wrow[0:1, :], statg_sb[0:1, 0:8], mrow[0:1, 0:8], ALU.mult))
            t_wrpm = gs() and opn("t", [t_wm, war.get("pb7")], lambda e: e.matmul(
                PB(7)[:, 0:8], ones1f[:, :], wrow[0:1, :], start=True, stop=True))
            t_wrp = gs() and opn("v", [t_wrpm], lambda e: e.tensor_copy(wrep[:, :], PB(7)[:, 0:8]))
            war["pb7"] = t_wrp
            t_slw = gs() and opn("v", [t_wrp, t_sel], lambda e: e.tensor_tensor(
                selw_bf.rearrange("p (a b) -> p a b", b=8),
                selbuf.rearrange("p (a b) -> p a b", b=8),
                wrep[:, None, :].broadcast_to([128, NTT, 8]), ALU.mult))
            t_str = None
            do_tr = gs()
            for tt in (range(NTT) if do_tr else []):
                t_str = opn("t", [t_slw, war.get("pb6"), war.get("pb7")] if tt == 0 else [],
                            lambda e, tt=tt: e.transpose(
                    PBbf(6, 2)[0:8, 128 * tt:128 * (tt + 1)], selw_bf[:, 8 * tt:8 * (tt + 1)],
                    idn_bf[:, :]))
            if do_tr:
                t_ste = opn("v", [t_str], lambda e: e.tensor_copy(selwT_sb[:, :], PBbf(6, 2)[0:8, 0:2048]))
                war["pb6"] = t_ste
                war["pb7"] = t_ste
                gst["t_ste"] = t_ste

        if "shared" in PHASES:
            t_sg_sl = dma("dg", [war.get("wo_last_mm")], lambda e: e.dma_start(
                out=sg_sl[...], in_=sgwT.rearrange("(dc p) e -> p dc e", p=128)), stream="sgx")
            t_su_sl = dma("dg", [], lambda e: e.dma_start(
                out=su_sl[...], in_=suwT.rearrange("(dc p) e -> p dc e", p=128)), stream="sux")
            last_gm = None
            hs_last = None
            for oc in range(NDC):
                gm4 = []
                for tg in range(4):
                    gm = None
                    for dc in range(NDC):
                        sl = slice(512 * tg, 512 * (tg + 1))
                        dep0 = ([t_sg_sl, ln2_tiles[4 * tg + 3] if ln2_tiles else None,
                                 war.get(f"pb{tg}")]) if (dc == 0) else []
                        gm = opn("t", dep0, lambda e, dc=dc, oc=oc, sl=sl, tg=tg: e.matmul(
                            PB(tg), sg_sl[:, dc, 128 * oc:128 * (oc + 1)], xn_fm[:, dc, sl],
                            start=(dc == 0), stop=(dc == NDC - 1)))
                    gm4.append(gm)
                gate_stage(2 * oc)
                tsl4 = []
                for tg in range(4):
                    tsl = opn("s", [gm4[tg], war.get("sil_r")], lambda e, tg=tg: e.activation(
                        sil4_bf[:, 512 * tg:512 * (tg + 1)], PB(tg), AF.Silu))
                    tsl4.append(tsl)
                    war["sil_r"] = tsl
                um4 = []
                for tg in range(4):
                    um = None
                    for dc in range(NDC):
                        sl = slice(512 * tg, 512 * (tg + 1))
                        dep0 = ([t_su_sl, tsl4[tg]]) if (dc == 0) else []
                        um = opn("t", dep0, lambda e, dc=dc, oc=oc, sl=sl, tg=tg: e.matmul(
                            PB(tg), su_sl[:, dc, 128 * oc:128 * (oc + 1)], xn_fm[:, dc, sl],
                            start=(dc == 0), stop=(dc == NDC - 1)))
                    um4.append(um)
                last_gm = um4[-1]
                for tg in range(4):
                    sl = slice(512 * tg, 512 * (tg + 1))
                    thv = opn("v", [um4[tg], tsl4[tg]], lambda e, oc=oc, sl=sl, tg=tg: e.tensor_tensor(
                        hs_sb[:, oc, sl], sil4_bf[:, 512 * tg:512 * (tg + 1)], PB(tg), ALU.mult))
                    war[f"pb{tg}"] = thv
                    war["sil_r"] = thv
                    hs_last = thv
                gate_stage(2 * oc + 1)
            gate_stage(NTT)
            gate_stage(NTT + 1)
            emit_gating_a1()
            t_sd_sl = dma("dg", [last_gm], lambda e: e.dma_start(
                out=sd_sl[...], in_=sdwT.rearrange("(dc p) e -> p dc e", p=128)), stream="sdx")
            prev_ev = {0: None, 1: None}
            last_sd = None
            for tt in range(NTT):
                mm = {}
                for half in range(2):
                    m = None
                    for sc in range(NDC):
                        m = opn("t", [t_sd_sl, hs_last, prev_ev[half], war.get(f"pb{2 + half}")]
                                if sc == 0 else [], lambda e, sc=sc, tt=tt, half=half: e.matmul(
                            PB(2 + half), hs_sb[:, sc, 128 * tt:128 * (tt + 1)],
                            sd_sl[:, sc, 512 * half:512 * (half + 1)],
                            start=(sc == 0), stop=(sc == NDC - 1)))
                    mm[half] = m
                    last_sd = m
                for half in range(2):
                    tev = opn("v", [mm[half]], lambda e, tt=tt, half=half: e.tensor_tensor(
                        x_tm[:, tt, 512 * half:512 * (half + 1)], PB(2 + half),
                        x_tm[:, tt, 512 * half:512 * (half + 1)], ALU.add))
                    prev_ev[half] = tev
                    war[f"pb{2 + half}"] = tev
                if tt == 2:
                    emit_gating_a2()
            war["sd_done"] = last_sd
            emit_gating_b()
        else:
            for _k in range(NTT + 2):
                gate_stage(_k)
            emit_gating_a1()
            emit_gating_a2()
            emit_gating_b()
        em.gate()

        if "experts" in PHASES:
            slab_seq = [(th, e_, icp) for th in range(2) for e_ in range(NE) for icp in range(2)]
            eg_toks = {}
            def load_expert_halfslab(i):
                th, e_, icp = slab_seq[i]
                par = i % 2
                deps = [war.get(("egslab", par)), war.get("edwar")]
                tg_ = dma("dg", deps, lambda e, e_=e_, icp=icp, par=par: e.dma_start(
                    out=eg_sl[par][...],
                    in_=egwT[e_].rearrange("(dc p) i -> p dc i", p=128)[:, :, 256 * icp:256 * (icp + 1)]),
                    stream=f"egx{par}")
                tu_ = dma("dg", deps, lambda e, e_=e_, icp=icp, par=par: e.dma_start(
                    out=eu_sl[par][...],
                    in_=euwT[e_].rearrange("(dc p) i -> p dc i", p=128)[:, :, 256 * icp:256 * (icp + 1)]),
                    stream=f"egx{par}")
                eg_toks[(th, e_, icp)] = (tu_, tu_)
            ed_war = None
            for th in range(2):
                load_expert_halfslab(th * 2 * NE + 0)
                load_expert_halfslab(th * 2 * NE + 1)
                tb_ = 1024 * th
                h_done = None
                for e_ in range(NE):
                    swb = selwb[e_ % 2]
                    t_srow = dma("dy", [gst.get("t_ste"), war.get("selwrow")],
                                 lambda e, e_=e_: e.dma_start(
                        out=selw_row[:, :], in_=selwT_sb[e_:e_ + 1, :]), stream="srw")
                    sw_mm = {}
                    for tg in range(2):
                        sw_mm[tg] = opn("t", [t_srow, war.get(f"pb{4 + tg}"), cl["o1b"]],
                                        lambda e, tg=tg, th=th: e.matmul(
                            PB(4 + tg), ones1b[:, :],
                            selw_row[0:1, tb_ + 512 * tg:tb_ + 512 * (tg + 1)],
                            start=True, stop=True))
                    t_swb = None
                    for tg in range(2):
                        t_swb = opn("s", [sw_mm[tg], war.get(("swb", e_ % 2))],
                                    lambda e, tg=tg, swb=swb: e.activation(
                            swb[:, 512 * tg:512 * (tg + 1)], PB(4 + tg), AF.Copy))
                        war[f"pb{4 + tg}"] = t_swb
                    war["selwrow"] = t_swb
                    war[("swb", e_ % 2)] = None
                    for icp in range(2):
                        gidx = (th * NE + e_) * 2 + icp
                        tg_sl, tu_sl = eg_toks[(th, e_, icp)]
                        par = gidx % 2
                        egb = eg_sl[par]; eub = eu_sl[par]
                        for ic_i in range(2):
                            ic = 2 * icp + ic_i
                            gm = um = None
                            for tg in range(2):
                                for dc in range(NDC):
                                    sl = slice(tb_ + 512 * tg, tb_ + 512 * (tg + 1))
                                    dep0 = [tg_sl, war.get("pb0"), war.get("pb1")] if (dc == 0 and tg == 0) else []
                                    gm = opn("t", dep0, lambda e, dc=dc, ic_i=ic_i, sl=sl,
                                             egb=egb, tg=tg: e.matmul(
                                        PB(tg), egb[:, dc, 128 * ic_i:128 * (ic_i + 1)],
                                        xn_fm[:, dc, sl], start=(dc == 0), stop=(dc == NDC - 1)))
                            for tg in range(2):
                                for dc in range(NDC):
                                    sl = slice(tb_ + 512 * tg, tb_ + 512 * (tg + 1))
                                    dep0 = [tu_sl, war.get("pb2"), war.get("pb3")] if (dc == 0 and tg == 0) else []
                                    um = opn("t", dep0, lambda e, dc=dc, ic_i=ic_i, sl=sl,
                                             eub=eub, tg=tg: e.matmul(
                                        PB(2 + tg), eub[:, dc, 128 * ic_i:128 * (ic_i + 1)],
                                        xn_fm[:, dc, sl], start=(dc == 0), stop=(dc == NDC - 1)))
                            for tg in range(2):
                                tsl = opn("s", [gm, war.get("sil_r")], lambda e, tg=tg: e.activation(
                                    sil_bf[:, :], PB(tg), AF.Silu))
                                war[f"pb{tg}"] = tsl
                                tsw = opn("g", [tsl, t_swb, war.get("silw_r")],
                                          lambda e, tg=tg, swb=swb: e.tensor_tensor(
                                    silw_bf[:, :], sil_bf[:, :],
                                    swb[:, 512 * tg:512 * (tg + 1)], ALU.mult))
                                war["sil_r"] = tsw
                                thv = opn("v", [tsw, um], lambda e, e_=e_, ic=ic, tg=tg: e.tensor_tensor(
                                    h_all[:, e_, ic, 512 * tg:512 * (tg + 1)],
                                    silw_bf[:, :], PB(2 + tg), ALU.mult))
                                war[f"pb{2 + tg}"] = thv
                                war["silw_r"] = thv
                                h_done = thv
                        war[("egslab", gidx % 2)] = h_done
                        if gidx + 2 < (th + 1) * 2 * NE:
                            load_expert_halfslab(gidx + 2)
                    war[("swb", e_ % 2)] = h_done
                def load_ed_round(hh, extra_deps):
                    par = hh % 2
                    tok = None
                    for e_ in range(NE):
                        dep = (extra_deps + [war.get(("edwar", par))]) if e_ == 0 else []
                        tok = dma("dg", dep, lambda e, e_=e_, hh=hh, par=par: e.dma_start(
                            out=ed_sl[par][e_][...],
                            in_=edwT[e_].rearrange("(ic p) h -> p ic h", p=128)[:, :, 256 * hh:256 * (hh + 1)]),
                            stream=f"edx{par}")
                    return tok
                first_dep = war.get("sd_done") if th == 0 else war.get("edwar")
                ed_tok = {0: load_ed_round(0, [first_dep, h_done])}
                for hh in range(4):
                    par = hh % 2
                    if hh + 1 < 4:
                        ed_tok[hh + 1] = load_ed_round(hh + 1, [])
                    prev_ev2 = {0: None, 1: None}
                    last_dn = None
                    for tt8 in range(8):
                        tt = 8 * th + tt8
                        pbk = tt8 % 2
                        m = None
                        first = True
                        for e_ in range(NE):
                            for ic in range(4):
                                deps0 = [ed_tok[hh]]
                                if first:
                                    deps0 += [h_done, prev_ev2[pbk], war.get(f"pb{2 + pbk}")]
                                m = opn("t", deps0 if (first or ic == 0) else [],
                                        lambda e, e_=e_, ic=ic, tt8=tt8, pbk=pbk, par=par,
                                        st=first, sp=(e_ == NE - 1 and ic == 3): e.matmul(
                                    PB(2 + pbk)[:, 0:256], h_all[:, e_, ic, 128 * tt8:128 * (tt8 + 1)],
                                    ed_sl[par][e_][:, ic, :], start=st, stop=sp))
                                first = False
                        tev = opn("v", [m], lambda e, tt=tt, hh=hh, pbk=pbk: e.tensor_tensor(
                            x_tm[:, tt, 256 * hh:256 * (hh + 1)], PB(2 + pbk)[:, 0:256],
                            x_tm[:, tt, 256 * hh:256 * (hh + 1)], ALU.add))
                        prev_ev2[pbk] = tev
                        war[f"pb{2 + pbk}"] = tev
                        last_dn = m
                    war[("edwar", par)] = last_dn
                    war["edwar"] = last_dn
        em.gate()

        t_st = dma("dy", [], lambda e: e.dma_start(
            out=out.rearrange("(tt p) d -> p tt d", p=128), in_=x_tm[:, :, :]), stream="sto")
        if em.cur == "y":
            em.wait(t_st)

    with nc.Block() as block:
        @block.tensor
        def _(eng):
            em.begin("t", eng); walk(); em.end()

        @block.vector
        def _(eng):
            em.begin("v", eng); walk(); em.end()

        @block.scalar
        def _(eng):
            em.begin("s", eng); walk(); em.end()

        @block.gpsimd
        def _(eng):
            em.begin("g", eng); walk(); em.end()

        @block.sync
        def _(eng):
            em.begin("y", eng); walk(); em.end()

    return nc


_NC_CACHE = None


def kernel(L0_attn_x, L0_attn_Wq, L0_attn_bq, L0_attn_Wk, L0_attn_bk,
           L0_attn_Wv, L0_attn_bv, L0_attn_Wo, L0_attn_bo,
           L0_ffn_gate_w, L0_ffn_lb_bias, L0_ffn_expert_gate_w,
           L0_ffn_expert_up_w, L0_ffn_expert_down_w,
           L0_ffn_shared_gate_w, L0_ffn_shared_up_w, L0_ffn_shared_down_w):
    global _NC_CACHE
    f32 = np.float32
    x = np.ascontiguousarray(np.asarray(L0_attn_x, dtype=f32))
    com = {
        "wqT": np.ascontiguousarray(np.asarray(L0_attn_Wq, f32).T),
        "wkT": np.ascontiguousarray(np.asarray(L0_attn_Wk, f32).T),
        "wvT": np.ascontiguousarray(np.asarray(L0_attn_Wv, f32).T),
        "woT": np.ascontiguousarray(np.asarray(L0_attn_Wo, f32).T),
        "bqc": np.ascontiguousarray(np.asarray(L0_attn_bq, f32).reshape(8, 128).T),
        "bkc": np.ascontiguousarray(np.asarray(L0_attn_bk, f32).reshape(8, 128).T),
        "bvr": np.ascontiguousarray(np.asarray(L0_attn_bv, f32).reshape(1, D)),
        "bvc": np.ascontiguousarray(np.asarray(L0_attn_bv, f32).reshape(8, 128).T),
        "bor": np.ascontiguousarray(np.asarray(L0_attn_bo, f32).reshape(1, D)),
        "gwT": np.ascontiguousarray(np.asarray(L0_ffn_gate_w, f32).T),
        "lb": np.ascontiguousarray(np.asarray(L0_ffn_lb_bias, f32).reshape(1, NE)),
        "egwT": np.ascontiguousarray(np.asarray(L0_ffn_expert_gate_w, f32).transpose(0, 2, 1)),
        "euwT": np.ascontiguousarray(np.asarray(L0_ffn_expert_up_w, f32).transpose(0, 2, 1)),
        "edwT": np.ascontiguousarray(np.asarray(L0_ffn_expert_down_w, f32).transpose(0, 2, 1)),
        "sgwT": np.ascontiguousarray(np.asarray(L0_ffn_shared_gate_w, f32).T),
        "suwT": np.ascontiguousarray(np.asarray(L0_ffn_shared_up_w, f32).T),
        "sdwT": np.ascontiguousarray(np.asarray(L0_ffn_shared_down_w, f32).T),
    }
    if _NC_CACHE is None:
        _NC_CACHE = build_program()
    nc = _NC_CACHE
    in_maps = [{"x": np.ascontiguousarray(x[i]), **com} for i in range(NCORES)]
    res = run_bass_kernel_spmd(nc, in_maps, core_ids=list(range(NCORES)))
    return np.stack([res.results[i]["out"] for i in range(NCORES)], axis=0)


# version 21
# speedup vs baseline: 1.0128x; 1.0128x over previous
import numpy as np
import ml_dtypes
import concourse.bass as bass
import concourse.mybir as mybir
from concourse.bass_utils import run_bass_kernel_spmd

F32 = mybir.dt.float32
BF16 = mybir.dt.bfloat16
I32 = mybir.dt.int32
AF = mybir.ActivationFunctionType
ALU = mybir.AluOpType
AX = mybir.AxisListType

T = 2048
D = 1024
NP = 8
NE = 8
EI = 512
SI = 1024
NTT = 16
NDC = 8
NCORES = 8

TWO_PI = 2 * np.pi
C1 = float(np.float32(TWO_PI))
C2P = float(np.float32(np.float64(np.float32(TWO_PI)) - TWO_PI))
INV_C1 = float(np.float32(1.0 / np.float64(np.float32(TWO_PI)))) 
PI_F = float(np.float32(np.pi))
PI_CLAMP = float(np.float32(3.1415925))
SHIFT_SIN = 0.0
SHIFT_COS = float(np.float32(np.pi / 2))

PHASES = {"tables", "ln1", "attn", "wo", "ln2", "gate", "shared", "experts"}
AR_WORLD = NCORES
GATE_STOP = 99
DBG = {}


class Chain:
    def __init__(self, nc, name):
        self.sem = nc.alloc_semaphore(name)
        self.n = 0


class Emitter:

    ENGINES = ("t", "v", "s", "g", "y")
    SELF_SERIAL = ("v", "s", "g")

    def __init__(self, nc):
        self.nc = nc
        self.ch = {e: Chain(nc, "c" + e) for e in self.ENGINES}
        self.streams = {}
        self.cur = None
        self.eng = None
        self.wm = {}
        self.totals = None

    def stream(self, name):
        if name not in self.streams:
            self.streams[name] = Chain(self.nc, "d_" + name)
        return self.streams[name]

    def begin(self, cur, eng):
        self.cur = cur
        self.eng = eng
        self.wm = {}
        for c in self.ch.values():
            c.n = 0
        for c in self.streams.values():
            c.n = 0

    def end(self):
        t = {k: c.n for k, c in self.ch.items()}
        t.update({"s_" + k: c.n for k, c in self.streams.items()})
        if self.totals is None:
            self.totals = t
        else:
            assert t == self.totals, f"pass divergence: {t} vs {self.totals}"

    def wait(self, tok):
        if tok is None or tok is False:
            return
        sem, val = tok
        if val <= 0:
            return
        key = sem.num
        if self.wm.get(key, 0) >= val:
            return
        self.eng.wait_ge(sem, val)
        self.wm[key] = val

    def op(self, engine, deps, fn):
        ch = self.ch[engine]
        if self.cur == engine:
            if engine in self.SELF_SERIAL and ch.n > 0:
                self.wait((ch.sem, ch.n))
            for d in deps:
                self.wait(d)
            inst = fn(self.eng)
            inst.then_inc(ch.sem, 1)
        ch.n += 1
        return (ch.sem, ch.n)

    def dma(self, queue, deps, fn, stream=None):
        ch = self.stream(stream if stream is not None else queue)
        issuer = "g" if queue == "dg" else "y"
        if self.cur == issuer:
            for d in deps:
                self.wait(d)
            inst = fn(self.eng)
            inst.then_inc(ch.sem, 16)
        ch.n += 16
        return (ch.sem, ch.n)

    def gate(self):
        toks = [(self.ch[e].sem, self.ch[e].n) for e in self.ENGINES]
        for tk in toks:
            self.wait(tk)


def build_program():
    nc = bass.Bass("TRN2", target_bir_lowering=False, debug=False)

    din = {}
    def I(name, shape):
        din[name] = nc.dram_tensor(name, list(shape), F32, kind="ExternalInput").ap()
        return din[name]

    x_in = I("x", (T, D))
    wqT = I("wqT", (D, D)); wkT = I("wkT", (D, D)); wvT = I("wvT", (D, D)); woT = I("woT", (D, D))
    I("bqc", (128, 8)); I("bkc", (128, 8))
    I("bvr", (1, D)); I("bor", (1, D)); I("bvc", (128, 8))
    gwT = I("gwT", (D, NE)); I("lb", (1, NE))
    egwT = I("egwT", (NE, D, EI)); euwT = I("euwT", (NE, D, EI)); edwT = I("edwT", (NE, EI, D))
    sgwT = I("sgwT", (D, SI)); suwT = I("suwT", (D, SI)); sdwT = I("sdwT", (SI, D))

    out = nc.dram_tensor("out", [T, D], F32, kind="ExternalOutput").ap()
    ar_in = nc.dram_tensor("ar_in", [1, 16], F32).ap()
    ar_out = nc.dram_tensor("ar_out", [1, 16], F32).ap()

    def inl(name, arr):
        return nc.inline_tensor(np.ascontiguousarray(arr), name=name).ap()

    idn_bf_c = inl("idn_bf", np.eye(128, dtype=ml_dtypes.bfloat16))
    idn_f_c = inl("idn_f", np.eye(128, dtype=np.float32))
    perm_np = np.zeros((128, 128), dtype=ml_dtypes.bfloat16)
    for i in range(128):
        perm_np[i ^ 32, i] = 1.0
    perm_c = inl("perm_bf", perm_np)
    tri_c = inl("tri_bf", (np.arange(128)[None, :] >= np.arange(128)[:, None]).astype(ml_dtypes.bfloat16))
    ones1b_c = inl("ones1_bf", np.ones((1, 128), dtype=ml_dtypes.bfloat16))
    ones1f_c = inl("ones1_f", np.ones((1, 128), dtype=np.float32))
    onescb_c = inl("onescol_bf", np.ones((128, 1), dtype=ml_dtypes.bfloat16))
    onescf_c = inl("onescol_f", np.ones((128, 1), dtype=np.float32))
    onesblk_c = inl("onesblk_f", np.ones((128, 64), dtype=np.float32))
    signc_c = inl("signcol", np.where((np.arange(128) % 64) < 32, -1.0, 1.0).astype(np.float32).reshape(128, 1))
    jcol_c = inl("jcol", (np.arange(128) % 32).astype(np.float32).reshape(128, 1))
    tbase_c = inl("tbase", np.broadcast_to(np.arange(T, dtype=np.float32), (128, T)))
    c10k_c = inl("c10k", np.full((1, 1), 10000.0, dtype=np.float32))

    base = (nc.sbuf_base + 31) // 32 * 32
    LIMIT = base + nc.sbuf_bytes_remaining - 512

    def AT(name, shape, dtype, off):
        nbytes = int(np.prod(shape[1:])) * mybir.dt.size(dtype)
        assert base + off + nbytes <= LIMIT, f"{name}: {off}+{nbytes} over SBUF"
        return nc.alloc_sbuf_tensor_at(name, list(shape), dtype, offset=base + off)

    class Bump:
        def __init__(self, off):
            self.o = off
        def al(self, name, shape, dtype):
            t_ = AT(name, shape, dtype, self.o)
            self.o += (int(np.prod(shape[1:])) * mybir.dt.size(dtype) + 31) // 32 * 32
            return t_

    P = Bump(0)
    x_tm = P.al("x_tm", [128, NTT, D], F32)
    xn_fm = P.al("xn_fm", [128, NDC, T], BF16)
    bqc = P.al("bqc", [128, 8], F32)
    bkc = P.al("bkc", [128, 8], F32)
    bvc = P.al("bvc", [128, 8], F32)
    gw_sl = P.al("gw_sl", [128, NDC, 8], F32)
    lb_sb = P.al("lb_sb", [1, 8], F32)
    lb_rep = P.al("lb_rep", [128, 8], F32)
    idn_bf = P.al("idn_bf_s", [128, 128], BF16)
    idn_f = P.al("idn_f_s", [128, 128], F32)
    perm_s = P.al("perm_s", [128, 128], BF16)
    tri_s = P.al("tri_s", [128, 128], BF16)
    ones1b = P.al("ones1b", [1, 128], BF16)
    ones1f = P.al("ones1f", [1, 128], F32)
    onescb = P.al("onescb", [128, 1], BF16)
    onescf = P.al("onescf", [128, 1], F32)
    onesblk = P.al("onesblk", [128, 64], F32)
    signc = P.al("signc", [128, 1], F32)
    jcol = P.al("jcol", [128, 1], F32)
    c10k = P.al("c10k", [1, 1], F32)
    eps_c = P.al("eps_c", [128, 1], F32)
    lnrow = P.al("lnrow", [1, 128], F32)
    lncol = P.al("lncol", [128, 1], F32)
    invf = P.al("invf", [128, 1], F32)
    st1 = P.al("st1", [128, 1], F32)
    st2 = P.al("st2", [128, 1], F32)
    st3 = P.al("st3", [128, 1], F32)
    stats_m = P.al("stats_m", [128, NTT], F32)
    bst = P.al("bst", [1, 4], F32)
    stats_r = P.al("stats_r", [128, NTT], F32)
    mrow = P.al("mrow", [1, 128], F32)
    mcol = P.al("mcol", [128, 2], F32)
    stats_sb = P.al("stats_sb", [1, 16], F32)
    statg_sb = P.al("statg_sb", [1, 16], F32)
    wrow = P.al("wrow", [1, 8], F32)
    wrep = P.al("wrep", [128, 8], F32)
    selwT_sb = P.al("selwT_sb", [8, T], BF16)
    selw_row_off = P.o
    selw_row = P.al("selw_row", [1, T], BF16)
    bvr = P.al("bvr", [1, D], BF16)
    PERS = P.o
    K_ = 1024
    AR0 = (PERS + 31) // 32 * 32

    ATT = AT("ATT", [128, NDC, T], BF16, AR0 + 0)
    cos_bf = AT("cos_bf", [128, T], BF16, AR0 + 32 * K_)
    sin_bf = AT("sin_bf", [128, T], BF16, AR0 + 36 * K_)
    qp0 = AT("qp0", [128, T], BF16, AR0 + 40 * K_)
    qp1 = AT("qp1", [128, T], BF16, AR0 + 44 * K_)
    kp0 = AT("kp0", [128, T], BF16, AR0 + 48 * K_)
    kp1 = AT("kp1", [128, T], BF16, AR0 + 52 * K_)
    vp0 = AT("vp0", [128, NTT, 128], BF16, AR0 + 56 * K_)
    vp1 = AT("vp1", [128, NTT, 128], BF16, AR0 + 60 * K_)
    rraw = AT("rraw", [128, 512], BF16, AR0 + 64 * K_)
    rsw = AT("rsw", [128, 512], BF16, AR0 + 65 * K_)
    rA = AT("rA", [128, 512], BF16, AR0 + 66 * K_)
    rB = AT("rB", [128, 512], BF16, AR0 + 67 * K_)
    ptA = AT("ptA", [128, 2048], BF16, AR0 + 68 * K_)
    ptB = AT("ptB", [128, 2048], BF16, selw_row_off)
    wq_sl = [AT(f"wq_sl{i}", [128, NDC, 128], BF16, AR0 + (72 + 2 * i) * K_) for i in range(2)]
    wk_sl = [AT(f"wk_sl{i}", [128, NDC, 128], BF16, AR0 + (76 + 2 * i) * K_) for i in range(2)]
    wv_sl = [AT(f"wv_sl{i}", [128, NDC, 128], BF16, AR0 + (80 + 2 * i) * K_) for i in range(2)]
    vraw = AT("vraw", [128, T], BF16, AR0 + 84 * K_)
    sqbuf = AT("sqbuf", [128, T], BF16, AR0 + 84 * K_)
    recb = AT("recb", [128, 512], F32, AR0 + 88 * K_)
    rlb_sb = AT("rlb_sb", [128, 512], F32, AR0 + 90 * K_)
    ATTN_END = AR0 + 94 * K_

    tabw1 = AT("tabw1", [128, T], F32, AR0 + 0)
    tabw2 = AT("tabw2", [128, T], F32, AR0 + 8 * K_)
    tabw3i = AT("tabw3i", [128, T], I32, AR0 + 16 * K_)
    kbufs = (qp0.bitcast(F32), kp0.bitcast(F32))
    sinws = (vp0.bitcast(F32), vp1.bitcast(F32))

    wo_sl = AT("wo_sl", [128, NDC, D], BF16, AR0 + 40 * K_)
    bor = AT("bor", [1, D], BF16, AR0 + 56 * K_)

    lnsq1 = AT("lnsq1", [128, D], F32, AR0 + 68 * K_)
    lnsq2 = AT("lnsq2", [128, D], F32, AR0 + 58 * K_)
    ropeC = AT("ropeC", [128, 512], BF16, AR0 + 92 * K_)
    ropeD = AT("ropeD", [128, 512], BF16, AR0 + 93 * K_)
    xnt_bf1 = AT("xnt_bf1", [128, D], BF16, selw_row_off)
    xnt_bf2a = AT("xnt_bf2a", [128, D], BF16, AR0 + 62 * K_)
    xnt_bf2b = AT("xnt_bf2b", [128, D], BF16, AR0 + 92 * K_)
    xnt_f = AT("xnt_f", [128, D], F32, AR0 + 64 * K_)
    xf_sb = AT("xf_sb", [128, NDC, 128], F32, AR0 + 68 * K_)

    logits_sb = AT("logits_sb", [128, NTT, 8], F32, AR0 + 72 * K_)
    scores_sb = AT("scores_sb", [128, NTT, 8], F32, AR0 + 73 * K_)
    selbuf = AT("selbuf", [128, 128], F32, AR0 + 74 * K_)
    wcbuf = AT("wcbuf", [128, 128], F32, AR0 + 75 * K_)
    sc2buf = AT("sc2buf", [128, 128], F32, AR0 + 76 * K_)
    m1buf = AT("m1buf", [128, NTT], F32, AR0 + 77 * K_)
    m2buf = AT("m2buf", [128, NTT], F32, AR0 + 77 * K_ + 128)
    rdenb = AT("rdenb", [128, NTT], F32, AR0 + 77 * K_ + 256)
    selw_bf = AT("selw_bf", [128, 128], BF16, AR0 + 77 * K_ + 512)

    sg_sl = AT("sg_sl", [128, NDC, SI], BF16, AR0 + 0)
    su_sl = AT("su_sl", [128, NDC, SI], BF16, AR0 + 16 * K_)
    hs_sb = AT("hs_sb", [128, NDC, T], BF16, AR0 + 32 * K_)
    sd_sl = AT("sd_sl", [128, NDC, D], BF16, AR0 + 0)

    h_all = AT("h_all", [128, NE, 4, 1024], BF16, AR0 + 0)
    eg_sl = [AT(f"eg_sl{i}", [128, NDC, 256], BF16, AR0 + (64 + 4 * i) * K_) for i in range(2)]
    eu_sl = [AT(f"eu_sl{i}", [128, NDC, 256], BF16, AR0 + (72 + 4 * i) * K_) for i in range(2)]
    selwb = [AT(f"selwb{i}", [128, T // 2], BF16, AR0 + 80 * K_ + 2 * K_ * i) for i in range(2)]
    sil_bf = AT("sil_bf", [128, 512], BF16, AR0 + 84 * K_)
    silw_bf = AT("silw_bf", [128, 512], BF16, AR0 + 85 * K_)
    sil4_bf = AT("sil4_bf", [128, 2048], BF16, AR0 + 84 * K_)
    ed_sl = [[AT(f"ed_sl{p}_{e}", [128, 4, 256], BF16, AR0 + (64 + 16 * p + 2 * e) * K_)
              for e in range(NE)] for p in range(2)]

    psA = nc.alloc_psum_tensor("psA", [128, 4096], F32)
    psA_bf = psA.bitcast(BF16)
    def PB(bank, banks=1):
        return psA[:, 512 * bank: 512 * (bank + banks)]
    def PBbf(bank, banks=1):
        return psA_bf[:, 1024 * bank: 1024 * (bank + banks)]

    DBG.clear()
    DBG.update(dict(x_tm=x_tm, xn_fm=xn_fm, cos_bf=cos_bf, sin_bf=sin_bf,
                    qp0=qp0, qp1=qp1, kp0=kp0, kp1=kp1, vp0=vp0, vp1=vp1,
                    ATT=ATT, logits_sb=logits_sb, scores_sb=scores_sb,
                    selbuf=selbuf, wrow=wrow, wrep=wrep, selwT_sb=selwT_sb,
                    m1buf=m1buf, m2buf=m2buf, h_all=h_all, hs_sb=hs_sb,
                    stats_sb=stats_sb, statg_sb=statg_sb, invf=invf, mcol=mcol))

    em = Emitter(nc)

    def walk():
        opn = em.op
        dma = em.dma
        war = {}

        tok_xc = []
        for xc, xst in enumerate(("ari", "aro", "mcl", "srw")):
            tok_xc.append(dma("dy", [], lambda e, xc=xc: e.dma_start(
                out=x_tm[:, 4 * xc:4 * (xc + 1), :],
                in_=x_in.rearrange("(tt p) d -> p tt d", p=128)[:, 4 * xc:4 * (xc + 1), :]),
                stream=xst))
        cl = {}
        for nm, dst, src in [
                ("idb", idn_bf, idn_bf_c), ("idf", idn_f, idn_f_c), ("perm", perm_s, perm_c),
                ("tri", tri_s, tri_c), ("o1b", ones1b, ones1b_c), ("o1f", ones1f, ones1f_c),
                ("ocb", onescb, onescb_c), ("ocf", onescf, onescf_c), ("obl", onesblk, onesblk_c),
                ("sgn", signc, signc_c), ("jc", jcol, jcol_c), ("c10", c10k, c10k_c),
                ("bqc", bqc, din["bqc"]), ("bkc", bkc, din["bkc"]),
                ("bvc", bvc, din["bvc"]), ("lb", lb_sb, din["lb"])]:
            cl[nm] = dma("dy", [], lambda e, d=dst, s=src: e.dma_start(out=d[...], in_=s), stream="pro")
        cl["gw"] = dma("dy", [], lambda e: e.dma_start(
            out=gw_sl[...], in_=gwT.rearrange("(dc p) e -> p dc e", p=128)), stream="pro")
        tok_tb = dma("dy", [], lambda e: e.dma_start(out=tabw1[:, :], in_=tbase_c), stream="pro")
        tok_bv = dma("dg", [], lambda e: e.dma_start(out=bvr[:, :], in_=din["bvr"]), stream="prog")

        slab_toks = {}
        def load_qkv_slabs(c):
            par = c % 2
            deps = [war.get(("qkvslab", par))]
            tq = dma("dg", deps, lambda e, c=c, par=par: e.dma_start(
                out=wq_sl[par][...],
                in_=wqT.rearrange("(dc p) e -> p dc e", p=128)[:, :, 128 * c:128 * (c + 1)]),
                stream=f"qkv{par}")
            tk = dma("dg", deps, lambda e, c=c, par=par: e.dma_start(
                out=wk_sl[par][...],
                in_=wkT.rearrange("(dc p) e -> p dc e", p=128)[:, :, 128 * c:128 * (c + 1)]),
                stream=f"qkv{par}")
            tv = dma("dg", deps, lambda e, c=c, par=par: e.dma_start(
                out=wv_sl[par][...],
                in_=wvT.rearrange("(dc p) e -> p dc e", p=128)[:, :, 128 * c:128 * (c + 1)]),
                stream=f"qkv{par}")
            slab_toks[c] = (tv, tv, tv)
        load_qkv_slabs(0)

        tok_pro = (em.stream("pro").sem, em.stream("pro").n)
        tok_x = tok_tb = tok_pro
        for _k in list(cl):
            cl[_k] = tok_pro
        tok_eps = opn("v", [], lambda e: e.memset(eps_c[:, :], 1e-5))
        tok_ps0 = opn("v", [], lambda e: e.memset(psA[:, 0:2048], 0.0))
        war["lnps0"] = tok_ps0
        war["lnps1"] = tok_ps0
        t_lbmm = opn("t", [cl["o1f"], cl["lb"], war.get("pb7")], lambda e: e.matmul(
            PB(7)[:, 0:8], ones1f[:, :], lb_sb[:, :], start=True, stop=True))
        tok_lbrep = opn("v", [t_lbmm], lambda e: e.tensor_copy(lb_rep[:, :], PB(7)[:, 0:8]))
        war["pb7"] = tok_lbrep


        if "tables" in PHASES:
            t_ln = opn("s", [cl["c10"]], lambda e: e.activation(lnrow[:, 0:1], c10k[:, :], AF.Ln))
            t_lnr = opn("v", [t_ln], lambda e: e.tensor_copy(
                lnrow[:, :], lnrow[0:1, 0:1].broadcast_to([1, 128])))
            t_lnc = dma("dy", [t_lnr], lambda e: e.dma_start(out=lncol[:, :], in_=lnrow[:, :]),
                        stream="lnc")
            t_ec = opn("v", [t_lnc, cl["jc"]], lambda e: e.tensor_tensor(
                st1[:, :], jcol[:, :], lncol[:, :], ALU.mult))
            t_iv = opn("s", [t_ec], lambda e: e.activation(
                invf[:, :], st1[:, :], AF.Exp, scale=float(-1.0 / 32.0)))
            t_ang = opn("v", [t_iv, tok_tb], lambda e: e.tensor_scalar_mul(
                tabw1[:, :], tabw1[:, :], invf[:, :]))

            def build_table(dst_bf, shift, sign_fold, deps):
                tu = opn("v", deps, lambda e: e.tensor_scalar(
                    tabw2[:, :], tabw1[:, :], shift, None, ALU.add))
                last = []
                for half in range(2):
                    sl = slice(1024 * half, 1024 * (half + 1))
                    kb = kbufs[half]; sw = sinws[half]
                    opn("v", [tu], lambda e, sl=sl, sw=sw: e.tensor_scalar(
                        sw[:, :], tabw2[:, sl], INV_C1, 0.5, ALU.mult, ALU.add))
                    opn("v", [], lambda e, sl=sl, sw=sw: e.tensor_copy(tabw3i[:, sl], sw[:, :]))
                    opn("v", [], lambda e, sl=sl, kb=kb: e.tensor_copy(kb[:, :], tabw3i[:, sl]))
                    opn("v", [], lambda e, sw=sw, kb=kb: e.tensor_tensor(
                        sw[:, :], kb[:, :], sw[:, :], ALU.is_gt))
                    opn("v", [], lambda e, sw=sw, kb=kb: e.tensor_tensor(
                        kb[:, :], kb[:, :], sw[:, :], ALU.subtract))
                    opn("v", [], lambda e, sl=sl, kb=kb: e.scalar_tensor_tensor(
                        tabw2[:, sl], kb[:, :], -C1, tabw2[:, sl], ALU.mult, ALU.add))
                    opn("v", [], lambda e, sl=sl, kb=kb: e.scalar_tensor_tensor(
                        tabw2[:, sl], kb[:, :], C2P, tabw2[:, sl], ALU.mult, ALU.add))
                    opn("v", [], lambda e, sl=sl: e.tensor_scalar(
                        tabw2[:, sl], tabw2[:, sl], PI_CLAMP, None, ALU.min))
                    c2t = opn("v", [], lambda e, sl=sl: e.tensor_scalar(
                        tabw2[:, sl], tabw2[:, sl], -PI_CLAMP, None, ALU.max))
                    tsn = opn("s", [c2t], lambda e, sl=sl, sw=sw: e.activation(
                        sw[:, :], tabw2[:, sl], AF.Sin))
                    if sign_fold:
                        tc = opn("v", [tsn, cl["sgn"]], lambda e, sl=sl, sw=sw: e.tensor_scalar_mul(
                            dst_bf[:, sl], sw[:, :], signc[:, :]))
                    else:
                        tc = opn("v", [tsn], lambda e, sl=sl, sw=sw: e.tensor_copy(
                            dst_bf[:, sl], sw[:, :]))
                    last.append(tc)
                return last

            ctoks = build_table(cos_bf, SHIFT_COS, False, [t_ang])
            stoks = build_table(sin_bf, SHIFT_SIN, True, [t_ang] + ctoks)
            tok_tables = stoks[-1]
        else:
            tok_tables = None


        def ln_tile(tt, second, lst, tdep=None):
            lnsq = lnsq2 if second else lnsq1
            xnt_bf = ((xnt_bf2a, xnt_bf2b)[tt % 2]) if second else xnt_bf1
            xt = x_tm[:, tt, :]
            tsq = opn("s", [war.get("lnsq_r"), tok_xc[tt // 4], tdep],
                      lambda e, xt=xt: e.activation(
                lnsq[:, :], xt, AF.Square))
            tss = opn("v", [tsq], lambda e: e.reduce_sum(st1[:, :], lnsq[:, :], axis=AX.X))
            war["lnsq_r"] = tss
            tms = opn("v", [tok_xc[tt // 4], tdep], lambda e, xt=xt: e.reduce_sum(st2[:, :], xt, axis=AX.X))
            opn("v", [], lambda e: e.tensor_scalar_mul(st2[:, :], st2[:, :], float(1.0 / D)))
            opn("v", [], lambda e: e.tensor_tensor(st3[:, :], st2[:, :], st2[:, :], ALU.mult))
            tva = opn("v", [], lambda e: e.scalar_tensor_tensor(
                st3[:, :], st1[:, :], float(1.0 / D), st3[:, :], ALU.mult, ALU.subtract))
            tsd = opn("s", [tva, tok_eps], lambda e: e.activation(
                st3[:, :], st3[:, :], AF.Sqrt, bias=eps_c[:, :]))
            trs = opn("v", [tsd], lambda e: e.reciprocal(st3[:, :], st3[:, :]))
            tap = opn("v", [war.get("xnt_r")], lambda e, xt=xt: e.tensor_scalar(
                xnt_bf[:, :], xt, st2[:, :], st3[:, :], ALU.subtract, ALU.mult))
            if second:
                opn("v", [war.get("stm_r")], lambda e, tt=tt: e.tensor_copy(
                    stats_m[:, tt:tt + 1], st2[:, :]))
                opn("v", [war.get("str_r")], lambda e, tt=tt: e.tensor_copy(
                    stats_r[:, tt:tt + 1], st3[:, :]))

            def part2():
                par = tt % 2
                pb = PBbf(5 + par) if second else PBbf(par)
                ttr = None
                for dc in range(NDC):
                    ttr = opn("t", [tap, lst["evac_war"][par], cl["idb"]] if dc == 0 else [],
                              lambda e, dc=dc, pb=pb: e.transpose(
                        pb[:, 128 * dc:128 * (dc + 1)], xnt_bf[:, 128 * dc:128 * (dc + 1)],
                        idn_bf[:, :]))
                war["xnt_r"] = ttr
                tev = opn("v", [ttr], lambda e, tt=tt, pb=pb: e.tensor_copy(
                    xn_fm[:, :, 128 * tt:128 * (tt + 1)],
                    pb.rearrange("p (dc q) -> p dc q", q=128)))
                lst["evac_war"][par] = tev
                return tev
            return part2

        def ln_state():
            return {"evac_war": {0: war.get("lnps0"), 1: war.get("lnps1")}}

        def ln_save(lst):
            war["lnps0"] = lst["evac_war"][0]
            war["lnps1"] = lst["evac_war"][1]

        def layernorm_pass(second, tile_deps=None):
            lst = ln_state()
            tile_toks = []
            for tt in range(NTT):
                tdep = tile_deps[tt] if tile_deps is not None else None
                tile_toks.append(ln_tile(tt, second, lst, tdep)())
            ln_save(lst)
            return tile_toks

        ln1_tiles = None
        if "ln1" in PHASES:
            ln1_tiles = layernorm_pass(second=False)
            war["pb0123"] = ln1_tiles[-1]

        if "attn" in PHASES:
            pair_state = {}

            def qkv_gen(c):
                par = c % 2
                wq = wq_sl[par]; wk = wk_sl[par]; wv = wv_sl[par]
                qbuf = (qp0, qp1)[par]; kbuf = (kp0, kp1)[par]; vbuf = (vp0, vp1)[par]
                tq_sl, tk_sl, tv_sl = slab_toks[c]
                qk_done = []
                for which, wsl, wtok, dstbuf, bias in (
                        ("q", wq, tq_sl, qbuf, bqc), ("k", wk, tk_sl, kbuf, bkc)):
                    for tg in range(4):
                        sl = slice(512 * tg, 512 * (tg + 1))
                        rr = (rraw, rsw)[tg % 2]
                        rs = (rA, rB)[tg % 2]
                        mm = None
                        deps0 = [wtok, war.get("pb7"), cl["bqc"], tok_tables,
                                 ln1_tiles[4 * tg + 3] if ln1_tiles else None]
                        for dc in range(NDC):
                            mm = opn("t", deps0 if dc == 0 else [],
                                     lambda e, dc=dc, wsl=wsl, sl=sl: e.matmul(
                                PB(7), wsl[:, dc, :], xn_fm[:, dc, sl],
                                start=(dc == 0), stop=(dc == NDC - 1)))
                        tev = opn("s", [mm, war.get(("rr_r1", tg % 2)), war.get(("rr_r2", tg % 2))],
                                  lambda e, bias=bias, c=c, rr=rr: e.activation(
                            rr[:, :], PB(7), AF.Identity, bias=bias[:, c:c + 1]))
                        yield
                        msw = opn("t", [tev, cl["perm"]], lambda e, rr=rr: e.matmul(
                            PB(7), perm_s[:, :], rr[:, :], start=True, stop=True))
                        war["pb7"] = msw
                        tsw = opn("s", [msw, war.get(("rs_r", tg % 2))],
                                  lambda e, rs=rs: e.activation(rs[:, :], PB(7), AF.Copy))
                        war["pb7"] = tsw
                        ta = opn("v", [tev], lambda e, sl=sl, rr=rr: e.tensor_tensor(
                            ropeC[:, :], rr[:, :], cos_bf[:, sl], ALU.mult))
                        war[("rr_r1", tg % 2)] = ta
                        war[("rr_r2", tg % 2)] = msw
                        tb = opn("g", [tsw, war.get("ropeD_r")],
                                 lambda e, sl=sl, rs=rs: e.tensor_tensor(
                            ropeD[:, :], rs[:, :], sin_bf[:, sl], ALU.mult))
                        war[("rs_r", tg % 2)] = tb
                        tc = opn("v", [ta, tb], lambda e, dstbuf=dstbuf, sl=sl: e.tensor_tensor(
                            dstbuf[:, sl], ropeC[:, :], ropeD[:, :], ALU.add))
                        war["ropeD_r"] = tc
                        qk_done.append(tc)
                        yield
                v_done = []
                tevs = {}
                for tg in range(4):
                    sl = slice(512 * tg, 512 * (tg + 1))
                    mm = None
                    deps0 = [tv_sl, war.get("pb7"), qk_done[-1]]
                    for dc in range(NDC):
                        mm = opn("t", deps0 if dc == 0 else [],
                                 lambda e, dc=dc, sl=sl, wv=wv: e.matmul(
                            PB(7), wv[:, dc, :], xn_fm[:, dc, sl],
                            start=(dc == 0), stop=(dc == NDC - 1)))
                    tevs[tg] = opn("s", [mm, war.get("vraw_r")], lambda e, sl=sl, c=c: e.activation(
                        vraw[:, sl], PB(7), AF.Identity, bias=bvc[:, c:c + 1]))
                    war["pb7"] = tevs[tg]
                    yield
                for half in range(2):
                    ttr = None
                    for j in range(NDC):
                        dep0 = [tevs[2 * half], tevs[2 * half + 1], war.get("pb7")] \
                            if j == 0 else []
                        ttr = opn("t", dep0, lambda e, half=half, j=j: e.transpose(
                            PBbf(7)[:, 128 * j:128 * (j + 1)],
                            vraw[:, 1024 * half + 128 * j:1024 * half + 128 * (j + 1)],
                            idn_bf[:, :]))
                    evv = opn("v", [ttr], lambda e, half=half, vbuf=vbuf: e.tensor_copy(
                        vbuf[:, 8 * half:8 * (half + 1), :],
                        PBbf(7).rearrange("p (a b) -> p a b", b=128)))
                    war["pb7"] = evv
                    for _ in range(8):
                        v_done.append(evv)
                    yield
                nrm = []
                for nm, buf, qtok in (("q", qbuf, qk_done[3]), ("k", kbuf, qk_done[7])):
                    tqs = opn("s", [qtok, war.get("sq_r"), evv],
                              lambda e, buf=buf: e.activation(sqbuf[:, :], buf[:, :], AF.Square))
                    dst = bst[0:1, 0:1] if nm == "q" else bst[0:1, 1:2]
                    run = None
                    for tg in range(4):
                        mmn = opn("t", [tqs, war.get("pb7")],
                                  lambda e, tg=tg: e.matmul(
                            PB(7)[0:1, :], onescb[:, :],
                            sqbuf[:, 512 * tg:512 * (tg + 1)], start=True, stop=True))
                        tm1 = opn("v", [mmn, run], lambda e: e.reduce_max(
                            bst[0:1, 2:3], PB(7)[0:1, :], axis=AX.X))
                        war["pb7"] = tm1
                        if tg == 0:
                            run = opn("v", [tm1], lambda e, dst=dst: e.tensor_copy(
                                dst, bst[0:1, 2:3]))
                        else:
                            run = opn("v", [tm1], lambda e, dst=dst: e.tensor_tensor(
                                dst, dst, bst[0:1, 2:3], ALU.max))
                    war["sq_r"] = run
                    nrm.append(run)
                    yield
                tpd = opn("v", [nrm[1]], lambda e: e.tensor_tensor(
                    bst[0:1, 0:1], bst[0:1, 0:1], bst[0:1, 1:2], ALU.add))
                tmr = opn("v", [tpd, war.get("mrow_r")], lambda e: e.tensor_scalar(
                    mrow[:, :], bst[0:1, 0:1].broadcast_to([1, 128]), float(-1.0 / 16.0),
                    None, ALU.mult))
                tmc = dma("dy", [tmr, war.get(("mcol_r", par))], lambda e, par=par: e.dma_start(
                    out=mcol[:, par:par + 1], in_=mrow[:, :]), stream="mcl")
                war["mrow_r"] = tmc
                pair_state[c] = dict(qbuf=qbuf, kbuf=kbuf, vbuf=vbuf, qk_done=qk_done,
                                     v_done=v_done, tmc=tmc, par=par)

            def drive(gen):
                if gen is None:
                    return None
                try:
                    next(gen)
                    return gen
                except StopIteration:
                    return None

            gen = qkv_gen(0)
            while drive(gen):
                pass
            next_gen = None

            for c in range(NP):
                par = c % 2
                st_ = pair_state[c]
                qbuf = st_["qbuf"]; kbuf = st_["kbuf"]; vbuf = st_["vbuf"]
                qk_done = st_["qk_done"]; v_done = st_["v_done"]; tmc = st_["tmc"]
                if c + 1 < NP:
                    load_qkv_slabs(c + 1)
                    next_gen = qkv_gen(c + 1)

                for qg in range(4):
                    ktmax = 4 * qg + 4
                    ngrp = ktmax // 2
                    exp_war = {0: war.get("ptavA"), 1: war.get("ptavB")}
                    sb_war = [war.get("pb0123"), war.get("pb2345"),
                              war.get("pb0"), war.get("pb1"), war.get("pb2"), war.get("pb3")]
                    tok_av = None
                    tok_deno = None
                    last_exp = None
                    pend = {}

                    def emit_scores_group(g):
                        nonlocal last_exp, sb_war
                        ptb = (ptA, ptB)[g % 2]
                        info = []
                        first = True
                        for kt in (2 * g, 2 * g + 1):
                            spar = kt - 2 * g
                            sps = PB(0, 2) if spar == 0 else PB(2, 2)
                            d = kt - 4 * qg
                            col0 = 128 * d if d >= 0 else 0
                            qlo = 512 * qg + col0
                            qn = 512 - col0
                            dep_s = ([qk_done[-1], exp_war[g % 2]] + sb_war) if first else []
                            first = False
                            opn("t", dep_s, lambda e, kt=kt, qlo=qlo, qn=qn, col0=col0,
                                sps=sps: e.matmul(
                                sps[:, col0:col0 + qn], kbuf[0:64, 128 * kt:128 * (kt + 1)],
                                qbuf[0:64, qlo:qlo + qn], start=True, stop=True))
                            ms2 = opn("t", [], lambda e, kt=kt, qlo=qlo, qn=qn, col0=col0,
                                      sps=sps: e.matmul(
                                sps[:, 512 + col0:512 + col0 + qn],
                                kbuf[64:128, 128 * kt:128 * (kt + 1)],
                                qbuf[64:128, qlo:qlo + qn], start=True, stop=True))
                            info.append((kt, spar, col0, qn, ms2))
                        e1 = opn("s", [info[-1][-1], tmc], lambda e, ptb=ptb, par=par: e.activation(
                            ptb[:, 0:2048], psA[:, 0:2048], AF.Exp,
                            bias=mcol[:, par:par + 1], scale=0.125))
                        pt_ready = [e1]
                        last_exp = e1
                        for kt, spar, col0, qn, _ in info:
                            if kt - 4 * qg >= 0:
                                pcol = 1024 * spar
                                m1_ = opn("v", [e1, cl["tri"]], lambda e, col0=col0,
                                          ptb=ptb, pcol=pcol: e.tensor_tensor(
                                    ptb[:, pcol + col0:pcol + col0 + 128],
                                    ptb[:, pcol + col0:pcol + col0 + 128],
                                    tri_s[:, :], ALU.mult))
                                m2_ = opn("v", [m1_], lambda e, col0=col0, ptb=ptb,
                                          pcol=pcol: e.tensor_tensor(
                                    ptb[:, pcol + 512 + col0:pcol + 512 + col0 + 128],
                                    ptb[:, pcol + 512 + col0:pcol + 512 + col0 + 128],
                                    tri_s[:, :], ALU.mult))
                                pt_ready += [m1_, m2_]
                                last_exp = m2_ if False else last_exp
                        sb_war = [last_exp]
                        pend[g] = (pt_ready, info, ptb)

                    def emit_avdeno_group(g, av_start, av_stop):
                        nonlocal tok_av, tok_deno
                        pt_ready, info, ptb = pend.pop(g)
                        first = True
                        for kt, spar, col0, qn, _ in info:
                            pcol = 1024 * spar
                            st = av_start and first
                            sp = av_stop and (kt == info[-1][0])
                            av_dep = (pt_ready + [v_done[info[-1][0]],
                                      war.get("pb4") if st else None,
                                      war.get("pb5") if st else None]) if first else []
                            first = False
                            opn("t", av_dep, lambda e, kt=kt, col0=col0, qn=qn, ptb=ptb,
                                pcol=pcol, st=st, sp=sp: e.matmul(
                                PB(4)[0:64, col0:col0 + qn], vbuf[:, kt, 0:64],
                                ptb[:, pcol + col0:pcol + col0 + qn], start=st, stop=sp,
                                tile_position=(0, 0)))
                            opn("t", [], lambda e, kt=kt, col0=col0, qn=qn, ptb=ptb,
                                pcol=pcol, st=st, sp=sp: e.matmul(
                                PB(4)[64:128, col0:col0 + qn], vbuf[:, kt, 64:128],
                                ptb[:, pcol + 512 + col0:pcol + 512 + col0 + qn],
                                start=st, stop=sp, tile_position=(0, 64)))
                            opn("t", [], lambda e, col0=col0, qn=qn, ptb=ptb, pcol=pcol,
                                st=st, sp=sp: e.matmul(
                                PB(5)[0:1, col0:col0 + qn], onescb[:, :],
                                ptb[:, pcol + col0:pcol + col0 + qn], start=st, stop=sp,
                                tile_position=(0, 0)))
                            d2 = opn("t", [], lambda e, col0=col0, qn=qn, ptb=ptb,
                                     pcol=pcol, st=st, sp=sp: e.matmul(
                                PB(5)[32:33, col0:col0 + qn], onescb[:, :],
                                ptb[:, pcol + 512 + col0:pcol + 512 + col0 + qn],
                                start=st, stop=sp, tile_position=(0, 32)))
                            tok_av = d2
                            tok_deno = d2
                        exp_war[g % 2] = tok_deno

                    for g in range(ngrp):
                        emit_scores_group(g)
                        next_gen = drive(next_gen)
                        if g == 0 and war.get("pending_epi") is not None:
                            war.pop("pending_epi")()
                        if g > 0:
                            emit_avdeno_group(g - 1, av_start=(g == 1), av_stop=False)
                            next_gen = drive(next_gen)
                    emit_avdeno_group(ngrp - 1, av_start=(ngrp == 1), av_stop=True)
                    war["ptavA"] = exp_war[0]
                    war["ptavB"] = exp_war[1]
                    war["pb0123"] = sb_war[0]
                    war[("mcol_r", par)] = last_exp
                    last_exp_of_pair = last_exp

                    def make_epilogue(c=c, qg=qg, tok_av=tok_av, tok_deno=tok_deno):
                        def epi():
                            r1 = opn("s", [tok_deno, war.get("recb_r")], lambda e: e.activation(
                                recb[0:1, :], PB(5)[0:1, :], AF.Copy))
                            r2 = opn("s", [r1], lambda e: e.activation(
                                recb[32:33, :], PB(5)[32:33, :], AF.Copy))
                            war["pb5"] = r2
                            b1 = opn("t", [r1, r2, war.get("pb6"), cl["obl"]], lambda e: e.matmul(
                                PB(6)[0:64, :], onesblk[0:1, :], recb[0:1, :], start=True, stop=True,
                                tile_position=(0, 0)))
                            b2 = opn("t", [], lambda e: e.matmul(
                                PB(6)[64:128, :], onesblk[32:33, :], recb[32:33, :], start=True,
                                stop=True, tile_position=(32, 64)))
                            war["recb_r"] = b2
                            rv = opn("v", [b2], lambda e: e.reciprocal(rlb_sb[:, :], PB(6)))
                            war["pb6"] = rv
                            ev = opn("v", [rv, tok_av], lambda e: e.tensor_tensor(
                                ATT[:, c, 512 * qg:512 * (qg + 1)], PB(4), rlb_sb[:, :], ALU.mult))
                            war["pb4"] = ev
                        return epi
                    war["pending_epi"] = make_epilogue()
                while drive(next_gen):
                    pass
                next_gen = None
        if war.get("pending_epi") is not None:
            war.pop("pending_epi")()
        em.gate()

        if "wo" in PHASES:
            dma("dg", [], lambda e: e.dma_start(
                out=wo_sl[...], in_=woT.rearrange("(dc p) e -> p dc e", p=128)), stream="wox")
            tok_bo = dma("dg", [], lambda e: e.dma_start(out=bor[:, :], in_=din["bor"]),
                         stream="wox")
            tok_wo = tok_bo
            prev_ev = {0: None, 1: None}
            wo_last_mm = None
            ln2_tiles = [] if "ln2" in PHASES else None
            lnst = ln_state()
            pend_p2 = None
            for tt in range(NTT):
                mm = {}
                for half in range(2):
                    m = None
                    for dc in range(NDC):
                        m = opn("t", [tok_wo, tok_bo, prev_ev[half]] if dc == 0 else [],
                                lambda e, dc=dc, tt=tt, half=half: e.matmul(
                            PB(half), ATT[:, dc, 128 * tt:128 * (tt + 1)],
                            wo_sl[:, dc, 512 * half:512 * (half + 1)],
                            start=(dc == 0), stop=False))
                    m = opn("t", [], lambda e, half=half: e.matmul(
                        PB(half), ones1b[:, :], bor[0:1, 512 * half:512 * (half + 1)],
                        start=False, stop=True))
                    mm[half] = m
                    wo_last_mm = m
                for half in range(2):
                    tev = opn("v", [mm[half]], lambda e, tt=tt, half=half: e.tensor_tensor(
                        x_tm[:, tt, 512 * half:512 * (half + 1)], PB(half),
                        x_tm[:, tt, 512 * half:512 * (half + 1)], ALU.add))
                    prev_ev[half] = tev
                if ln2_tiles is not None:
                    p2 = ln_tile(tt, True, lnst, tdep=prev_ev[1])
                    if pend_p2 is not None:
                        ln2_tiles.append(pend_p2())
                    pend_p2 = p2
            if ln2_tiles is not None and pend_p2 is not None:
                ln2_tiles.append(pend_p2())
            ln_save(lnst)
            war["pb0"] = prev_ev[0]
            war["pb1"] = prev_ev[1]
            war["wo_last_mm"] = wo_last_mm
        else:
            ln2_tiles = None
            if "ln2" in PHASES:
                ln2_tiles = layernorm_pass(second=True)
        em.gate()

        gst = {"t_ste": None}
        GS = [0]
        def gs():
            GS[0] += 1
            return GS[0] <= GATE_STOP

        gp = {"tapf": {}, "tfr": {}, "tfe": {}, "tgm": {}, "tge": {}}
        def gate_stage(k):
            if "ln2" not in PHASES:
                return
            j = k - 2
            if 0 <= j < NTT:
                tgm = None
                for dc in range(NDC):
                    tgm = opn("t", [gp["tfe"][j], cl["gw"], gp["tge"].get(j - 1)] if dc == 0 else [],
                              lambda e, dc=dc: e.matmul(
                        PB(6)[:, 0:8], xf_sb[:, dc, :], gw_sl[:, dc, :],
                        start=(dc == 0), stop=(dc == NDC - 1)))
                gp["tgm"][j] = tgm
                gp["tge"][j] = opn("v", [tgm], lambda e, j=j: e.tensor_copy(
                    logits_sb[:, j, :], PB(6)[:, 0:8]))
                if j == NTT - 1:
                    war["lngate"] = gp["tge"][j]
            j = k - 1
            if 0 <= j < NTT:
                tfr = None
                for dc in range(NDC):
                    tfr = opn("t", [gp["tapf"][j], cl["idf"], gp["tfe"].get(j - 1)] if dc == 0 else [],
                              lambda e, dc=dc: e.transpose(
                        PB(4, 2)[:, 128 * dc:128 * (dc + 1)],
                        xnt_f[:, 128 * dc:128 * (dc + 1)], idn_f[:, :]))
                gp["tfr"][j] = tfr
                gp["tfe"][j] = opn("v", [tfr, gp["tgm"].get(j - 1)], lambda e: e.tensor_copy(
                    xf_sb[:, :, :], PB(4, 2).rearrange("p (dc q) -> p dc q", q=128)))
            if k < NTT:
                j = k
                gp["tapf"][j] = opn("v", [gp["tfr"].get(j - 1)], lambda e, j=j: e.tensor_scalar(
                    xnt_f[:, :], x_tm[:, j, :], stats_m[:, j:j + 1], stats_r[:, j:j + 1],
                    ALU.subtract, ALU.mult))

        def emit_gating_a1():
            if "gate" not in PHASES:
                return
            sc3 = scores_sb.rearrange("p a b -> p (a b)")
            lg3 = logits_sb.rearrange("p a b -> p (a b)")
            t_lb = gs() and opn("v", [tok_lbrep, war.get("lngate")], lambda e: e.tensor_tensor(
                logits_sb[:, :, :], logits_sb[:, :, :],
                lb_rep[:, None, :].broadcast_to([128, NTT, 8]), ALU.add))
            t_sg = gs() and opn("s", [t_lb], lambda e: e.activation(sc3[:, :], lg3[:, :], AF.Sigmoid))
            t_m1 = gs() and opn("v", [t_sg], lambda e: e.reduce_max(m1buf[:, :], scores_sb[:, :, :], axis=AX.X))
            t_t1 = gs() and opn("v", [t_m1], lambda e: e.tensor_tensor(
                selbuf.rearrange("p (a b) -> p a b", b=8), scores_sb[:, :, :],
                m1buf[:, :, None].broadcast_to([128, NTT, 8]), ALU.is_ge))
            t_s2 = gs() and opn("v", [t_t1], lambda e: e.scalar_tensor_tensor(
                sc2buf[:, :], selbuf[:, :], -1e30, sc3[:, :], ALU.mult, ALU.add))
            t_m2 = gs() and opn("v", [t_s2], lambda e: e.reduce_max(
                m2buf[:, :], sc2buf.rearrange("p (a b) -> p a b", b=8), axis=AX.X))
            t_sel = gs() and opn("v", [t_m2], lambda e: e.tensor_tensor(
                selbuf.rearrange("p (a b) -> p a b", b=8), scores_sb[:, :, :],
                m2buf[:, :, None].broadcast_to([128, NTT, 8]), ALU.is_ge))
            gs() and opn("v", [t_m2], lambda e: e.tensor_tensor(rdenb[:, :], m1buf[:, :], m2buf[:, :], ALU.add))
            gs() and opn("v", [], lambda e: e.tensor_scalar(rdenb[:, :], rdenb[:, :], 1e-8, None, ALU.add))
            t_rd = gs() and opn("v", [], lambda e: e.reciprocal(rdenb[:, :], rdenb[:, :]))
            t_wc = gs() and opn("v", [t_sel, t_rd], lambda e: e.tensor_tensor(
                wcbuf[:, :], selbuf[:, :], sc3[:, :], ALU.mult))
            t_wc2 = gs() and opn("v", [t_wc], lambda e: e.tensor_tensor(
                wcbuf.rearrange("p (a b) -> p a b", b=8),
                wcbuf.rearrange("p (a b) -> p a b", b=8),
                rdenb[:, :, None].broadcast_to([128, NTT, 8]), ALU.mult))
            gst["t_wc2"] = t_wc2
            gst["t_sel"] = t_sel

        def emit_gating_a2():
            if "gate" not in PHASES:
                return
            t_wc2 = gst.get("t_wc2")
            t_sel = gst.get("t_sel")
            t_sS = gs() and opn("t", [t_wc2, war.get("pb7"), cl["ocf"]], lambda e: e.matmul(
                PB(7)[0:1, 0:128], onescf[:, :], wcbuf[:, :], start=True, stop=True))
            t_sC = gs() and opn("t", [t_sel], lambda e: e.matmul(
                PB(7)[0:1, 128:256], onescf[:, :], selbuf[:, :], start=True, stop=True))
            t_cp = gs() and opn("v", [t_sS, t_sC], lambda e: e.tensor_copy(
                rlb_sb[0:1, 0:256], PB(7)[0:1, 0:256]))
            war["pb7"] = t_cp
            t_tr = t_cp
            for w in (64, 32, 16, 8):
                t_tr = gs() and opn("v", [t_tr], lambda e, w=w: e.tensor_tensor(
                    rlb_sb[0:1, 0:w], rlb_sb[0:1, 0:w], rlb_sb[0:1, w:2 * w], ALU.add))
                t_tr = gs() and opn("v", [t_tr], lambda e, w=w: e.tensor_tensor(
                    rlb_sb[0:1, 128:128 + w], rlb_sb[0:1, 128:128 + w],
                    rlb_sb[0:1, 128 + w:128 + 2 * w], ALU.add))
            t_db = gs() and dma("dy", [t_tr], lambda e: e.dma_start(
                out=ar_in,
                in_=rlb_sb[0:1, 0:256].rearrange("o (a b) -> o a b", b=128)[:, :, 0:8]),
                stream="ari")
            ccch = em.stream("ccx")
            do_cc = gs()
            if do_cc and em.cur == "g":
                em.wait(t_db)
                e = em.eng
                e.collective_compute(
                    "AllReduce", ALU.add, replica_groups=[list(range(AR_WORLD))],
                    ins=[ar_in.opt()], outs=[ar_out.opt()]).then_inc(ccch.sem, 1)
            if do_cc:
                ccch.n += 1
            t_cc = (ccch.sem, ccch.n) if do_cc else None
            gst["t_cc"] = t_cc

        def emit_gating_b():
            if "gate" not in PHASES:
                return
            t_cc = gst.get("t_cc")
            t_sel = gst.get("t_sel")
            t_gb = gs() and dma("dy", [t_cc], lambda e: e.dma_start(out=statg_sb[:, :], in_=ar_out),
                       stream="aro")
            t_cm = gs() and opn("v", [t_gb], lambda e: e.tensor_scalar_max(
                mrow[0:1, 0:8], statg_sb[0:1, 8:16], 1.0))
            t_rc = gs() and opn("v", [t_cm], lambda e: e.reciprocal(mrow[0:1, 0:8], mrow[0:1, 0:8]))
            t_wm = gs() and opn("v", [t_rc], lambda e: e.tensor_tensor(
                wrow[0:1, :], statg_sb[0:1, 0:8], mrow[0:1, 0:8], ALU.mult))
            t_wrpm = gs() and opn("t", [t_wm, war.get("pb7")], lambda e: e.matmul(
                PB(7)[:, 0:8], ones1f[:, :], wrow[0:1, :], start=True, stop=True))
            t_wrp = gs() and opn("v", [t_wrpm], lambda e: e.tensor_copy(wrep[:, :], PB(7)[:, 0:8]))
            war["pb7"] = t_wrp
            t_slw = gs() and opn("v", [t_wrp, t_sel], lambda e: e.tensor_tensor(
                selw_bf.rearrange("p (a b) -> p a b", b=8),
                selbuf.rearrange("p (a b) -> p a b", b=8),
                wrep[:, None, :].broadcast_to([128, NTT, 8]), ALU.mult))
            t_str = None
            do_tr = gs()
            for tt in (range(NTT) if do_tr else []):
                t_str = opn("t", [t_slw, war.get("pb6"), war.get("pb7")] if tt == 0 else [],
                            lambda e, tt=tt: e.transpose(
                    PBbf(6, 2)[0:8, 128 * tt:128 * (tt + 1)], selw_bf[:, 8 * tt:8 * (tt + 1)],
                    idn_bf[:, :]))
            if do_tr:
                t_ste = opn("v", [t_str], lambda e: e.tensor_copy(selwT_sb[:, :], PBbf(6, 2)[0:8, 0:2048]))
                war["pb6"] = t_ste
                war["pb7"] = t_ste
                gst["t_ste"] = t_ste

        if "shared" in PHASES:
            t_sg_sl = dma("dg", [war.get("wo_last_mm")], lambda e: e.dma_start(
                out=sg_sl[...], in_=sgwT.rearrange("(dc p) e -> p dc e", p=128)), stream="sgx")
            t_su_sl = dma("dg", [], lambda e: e.dma_start(
                out=su_sl[...], in_=suwT.rearrange("(dc p) e -> p dc e", p=128)), stream="sux")
            last_gm = None
            hs_last = None
            for oc in range(NDC):
                gm4 = []
                for tg in range(4):
                    gm = None
                    for dc in range(NDC):
                        sl = slice(512 * tg, 512 * (tg + 1))
                        dep0 = ([t_sg_sl, ln2_tiles[4 * tg + 3] if ln2_tiles else None,
                                 war.get(f"pb{tg}")]) if (dc == 0) else []
                        gm = opn("t", dep0, lambda e, dc=dc, oc=oc, sl=sl, tg=tg: e.matmul(
                            PB(tg), sg_sl[:, dc, 128 * oc:128 * (oc + 1)], xn_fm[:, dc, sl],
                            start=(dc == 0), stop=(dc == NDC - 1)))
                    gm4.append(gm)
                gate_stage(2 * oc)
                tsl4 = []
                for tg in range(4):
                    tsl = opn("s", [gm4[tg], war.get("sil_r")], lambda e, tg=tg: e.activation(
                        sil4_bf[:, 512 * tg:512 * (tg + 1)], PB(tg), AF.Silu))
                    tsl4.append(tsl)
                    war["sil_r"] = tsl
                um4 = []
                for tg in range(4):
                    um = None
                    for dc in range(NDC):
                        sl = slice(512 * tg, 512 * (tg + 1))
                        dep0 = ([t_su_sl, tsl4[tg]]) if (dc == 0) else []
                        um = opn("t", dep0, lambda e, dc=dc, oc=oc, sl=sl, tg=tg: e.matmul(
                            PB(tg), su_sl[:, dc, 128 * oc:128 * (oc + 1)], xn_fm[:, dc, sl],
                            start=(dc == 0), stop=(dc == NDC - 1)))
                    um4.append(um)
                last_gm = um4[-1]
                for tg in range(4):
                    sl = slice(512 * tg, 512 * (tg + 1))
                    thv = opn("v", [um4[tg], tsl4[tg]], lambda e, oc=oc, sl=sl, tg=tg: e.tensor_tensor(
                        hs_sb[:, oc, sl], sil4_bf[:, 512 * tg:512 * (tg + 1)], PB(tg), ALU.mult))
                    war[f"pb{tg}"] = thv
                    war["sil_r"] = thv
                    hs_last = thv
                gate_stage(2 * oc + 1)
            gate_stage(NTT)
            gate_stage(NTT + 1)
            emit_gating_a1()
            t_sd_sl = dma("dg", [last_gm], lambda e: e.dma_start(
                out=sd_sl[...], in_=sdwT.rearrange("(dc p) e -> p dc e", p=128)), stream="sdx")
            prev_ev = {0: None, 1: None}
            last_sd = None
            for tt in range(NTT):
                mm = {}
                for half in range(2):
                    m = None
                    for sc in range(NDC):
                        m = opn("t", [t_sd_sl, hs_last, prev_ev[half], war.get(f"pb{2 + half}")]
                                if sc == 0 else [], lambda e, sc=sc, tt=tt, half=half: e.matmul(
                            PB(2 + half), hs_sb[:, sc, 128 * tt:128 * (tt + 1)],
                            sd_sl[:, sc, 512 * half:512 * (half + 1)],
                            start=(sc == 0), stop=(sc == NDC - 1)))
                    mm[half] = m
                    last_sd = m
                for half in range(2):
                    tev = opn("v", [mm[half]], lambda e, tt=tt, half=half: e.tensor_tensor(
                        x_tm[:, tt, 512 * half:512 * (half + 1)], PB(2 + half),
                        x_tm[:, tt, 512 * half:512 * (half + 1)], ALU.add))
                    prev_ev[half] = tev
                    war[f"pb{2 + half}"] = tev
                if tt == 2:
                    emit_gating_a2()
            war["sd_done"] = last_sd
            emit_gating_b()
        else:
            for _k in range(NTT + 2):
                gate_stage(_k)
            emit_gating_a1()
            emit_gating_a2()
            emit_gating_b()
        em.gate()

        if "experts" in PHASES:
            slab_seq = [(th, e_, icp) for th in range(2) for e_ in range(NE) for icp in range(2)]
            eg_toks = {}
            def load_expert_halfslab(i):
                th, e_, icp = slab_seq[i]
                par = i % 2
                deps = [war.get(("egslab", par)), war.get("edwar")]
                tg_ = dma("dg", deps, lambda e, e_=e_, icp=icp, par=par: e.dma_start(
                    out=eg_sl[par][...],
                    in_=egwT[e_].rearrange("(dc p) i -> p dc i", p=128)[:, :, 256 * icp:256 * (icp + 1)]),
                    stream=f"egx{par}")
                tu_ = dma("dg", deps, lambda e, e_=e_, icp=icp, par=par: e.dma_start(
                    out=eu_sl[par][...],
                    in_=euwT[e_].rearrange("(dc p) i -> p dc i", p=128)[:, :, 256 * icp:256 * (icp + 1)]),
                    stream=f"egx{par}")
                eg_toks[(th, e_, icp)] = (tu_, tu_)
            ed_war = None
            for th in range(2):
                load_expert_halfslab(th * 2 * NE + 0)
                load_expert_halfslab(th * 2 * NE + 1)
                tb_ = 1024 * th
                h_done = None
                for e_ in range(NE):
                    swb = selwb[e_ % 2]
                    t_srow = dma("dy", [gst.get("t_ste"), war.get("selwrow")],
                                 lambda e, e_=e_: e.dma_start(
                        out=selw_row[:, :], in_=selwT_sb[e_:e_ + 1, :]), stream="srw")
                    sw_mm = {}
                    for tg in range(2):
                        sw_mm[tg] = opn("t", [t_srow, war.get(f"pb{4 + tg}"), cl["o1b"]],
                                        lambda e, tg=tg, th=th: e.matmul(
                            PB(4 + tg), ones1b[:, :],
                            selw_row[0:1, tb_ + 512 * tg:tb_ + 512 * (tg + 1)],
                            start=True, stop=True))
                    t_swb = None
                    for tg in range(2):
                        t_swb = opn("s", [sw_mm[tg], war.get(("swb", e_ % 2))],
                                    lambda e, tg=tg, swb=swb: e.activation(
                            swb[:, 512 * tg:512 * (tg + 1)], PB(4 + tg), AF.Copy))
                        war[f"pb{4 + tg}"] = t_swb
                    war["selwrow"] = t_swb
                    war[("swb", e_ % 2)] = None
                    for icp in range(2):
                        gidx = (th * NE + e_) * 2 + icp
                        tg_sl, tu_sl = eg_toks[(th, e_, icp)]
                        par = gidx % 2
                        egb = eg_sl[par]; eub = eu_sl[par]
                        for ic_i in range(2):
                            ic = 2 * icp + ic_i
                            gm = um = None
                            for tg in range(2):
                                for dc in range(NDC):
                                    sl = slice(tb_ + 512 * tg, tb_ + 512 * (tg + 1))
                                    dep0 = [tg_sl, war.get("pb0"), war.get("pb1")] if (dc == 0 and tg == 0) else []
                                    gm = opn("t", dep0, lambda e, dc=dc, ic_i=ic_i, sl=sl,
                                             egb=egb, tg=tg: e.matmul(
                                        PB(tg), egb[:, dc, 128 * ic_i:128 * (ic_i + 1)],
                                        xn_fm[:, dc, sl], start=(dc == 0), stop=(dc == NDC - 1)))
                            for tg in range(2):
                                for dc in range(NDC):
                                    sl = slice(tb_ + 512 * tg, tb_ + 512 * (tg + 1))
                                    dep0 = [tu_sl, war.get("pb2"), war.get("pb3")] if (dc == 0 and tg == 0) else []
                                    um = opn("t", dep0, lambda e, dc=dc, ic_i=ic_i, sl=sl,
                                             eub=eub, tg=tg: e.matmul(
                                        PB(2 + tg), eub[:, dc, 128 * ic_i:128 * (ic_i + 1)],
                                        xn_fm[:, dc, sl], start=(dc == 0), stop=(dc == NDC - 1)))
                            for tg in range(2):
                                tsl = opn("s", [gm, war.get("sil_r")], lambda e, tg=tg: e.activation(
                                    sil_bf[:, :], PB(tg), AF.Silu))
                                war[f"pb{tg}"] = tsl
                                tsw = opn("g", [tsl, t_swb, war.get("silw_r")],
                                          lambda e, tg=tg, swb=swb: e.tensor_tensor(
                                    silw_bf[:, :], sil_bf[:, :],
                                    swb[:, 512 * tg:512 * (tg + 1)], ALU.mult))
                                war["sil_r"] = tsw
                                thv = opn("v", [tsw, um], lambda e, e_=e_, ic=ic, tg=tg: e.tensor_tensor(
                                    h_all[:, e_, ic, 512 * tg:512 * (tg + 1)],
                                    silw_bf[:, :], PB(2 + tg), ALU.mult))
                                war[f"pb{2 + tg}"] = thv
                                war["silw_r"] = thv
                                h_done = thv
                        war[("egslab", gidx % 2)] = h_done
                        if gidx + 2 < (th + 1) * 2 * NE:
                            load_expert_halfslab(gidx + 2)
                    war[("swb", e_ % 2)] = h_done
                def load_ed_round(hh, extra_deps):
                    par = hh % 2
                    tok = None
                    for e_ in range(NE):
                        dep = (extra_deps + [war.get(("edwar", par))]) if e_ == 0 else []
                        tok = dma("dg", dep, lambda e, e_=e_, hh=hh, par=par: e.dma_start(
                            out=ed_sl[par][e_][...],
                            in_=edwT[e_].rearrange("(ic p) h -> p ic h", p=128)[:, :, 256 * hh:256 * (hh + 1)]),
                            stream=f"edx{par}")
                    return tok
                first_dep = war.get("sd_done") if th == 0 else war.get("edwar")
                ed_tok = {0: load_ed_round(0, [first_dep, h_done])}
                for hh in range(4):
                    par = hh % 2
                    if hh + 1 < 4:
                        ed_tok[hh + 1] = load_ed_round(hh + 1, [])
                    prev_ev2 = {0: None, 1: None}
                    last_dn = None
                    for tt8 in range(8):
                        tt = 8 * th + tt8
                        pbk = tt8 % 2
                        m = None
                        first = True
                        for e_ in range(NE):
                            for ic in range(4):
                                deps0 = [ed_tok[hh]]
                                if first:
                                    deps0 += [h_done, prev_ev2[pbk], war.get(f"pb{2 + pbk}")]
                                m = opn("t", deps0 if (first or ic == 0) else [],
                                        lambda e, e_=e_, ic=ic, tt8=tt8, pbk=pbk, par=par,
                                        st=first, sp=(e_ == NE - 1 and ic == 3): e.matmul(
                                    PB(2 + pbk)[:, 0:256], h_all[:, e_, ic, 128 * tt8:128 * (tt8 + 1)],
                                    ed_sl[par][e_][:, ic, :], start=st, stop=sp))
                                first = False
                        tev = opn("v", [m], lambda e, tt=tt, hh=hh, pbk=pbk: e.tensor_tensor(
                            x_tm[:, tt, 256 * hh:256 * (hh + 1)], PB(2 + pbk)[:, 0:256],
                            x_tm[:, tt, 256 * hh:256 * (hh + 1)], ALU.add))
                        prev_ev2[pbk] = tev
                        war[f"pb{2 + pbk}"] = tev
                        last_dn = m
                    war[("edwar", par)] = last_dn
                    war["edwar"] = last_dn
        em.gate()

        t_st = dma("dy", [], lambda e: e.dma_start(
            out=out.rearrange("(tt p) d -> p tt d", p=128), in_=x_tm[:, :, :]), stream="sto")
        if em.cur == "y":
            em.wait(t_st)

    with nc.Block() as block:
        @block.tensor
        def _(eng):
            em.begin("t", eng); walk(); em.end()

        @block.vector
        def _(eng):
            em.begin("v", eng); walk(); em.end()

        @block.scalar
        def _(eng):
            em.begin("s", eng); walk(); em.end()

        @block.gpsimd
        def _(eng):
            em.begin("g", eng); walk(); em.end()

        @block.sync
        def _(eng):
            em.begin("y", eng); walk(); em.end()

    return nc


_NC_CACHE = None


def kernel(L0_attn_x, L0_attn_Wq, L0_attn_bq, L0_attn_Wk, L0_attn_bk,
           L0_attn_Wv, L0_attn_bv, L0_attn_Wo, L0_attn_bo,
           L0_ffn_gate_w, L0_ffn_lb_bias, L0_ffn_expert_gate_w,
           L0_ffn_expert_up_w, L0_ffn_expert_down_w,
           L0_ffn_shared_gate_w, L0_ffn_shared_up_w, L0_ffn_shared_down_w):
    global _NC_CACHE
    f32 = np.float32
    x = np.ascontiguousarray(np.asarray(L0_attn_x, dtype=f32))
    com = {
        "wqT": np.ascontiguousarray(np.asarray(L0_attn_Wq, f32).T),
        "wkT": np.ascontiguousarray(np.asarray(L0_attn_Wk, f32).T),
        "wvT": np.ascontiguousarray(np.asarray(L0_attn_Wv, f32).T),
        "woT": np.ascontiguousarray(np.asarray(L0_attn_Wo, f32).T),
        "bqc": np.ascontiguousarray(np.asarray(L0_attn_bq, f32).reshape(8, 128).T),
        "bkc": np.ascontiguousarray(np.asarray(L0_attn_bk, f32).reshape(8, 128).T),
        "bvr": np.ascontiguousarray(np.asarray(L0_attn_bv, f32).reshape(1, D)),
        "bvc": np.ascontiguousarray(np.asarray(L0_attn_bv, f32).reshape(8, 128).T),
        "bor": np.ascontiguousarray(np.asarray(L0_attn_bo, f32).reshape(1, D)),
        "gwT": np.ascontiguousarray(np.asarray(L0_ffn_gate_w, f32).T),
        "lb": np.ascontiguousarray(np.asarray(L0_ffn_lb_bias, f32).reshape(1, NE)),
        "egwT": np.ascontiguousarray(np.asarray(L0_ffn_expert_gate_w, f32).transpose(0, 2, 1)),
        "euwT": np.ascontiguousarray(np.asarray(L0_ffn_expert_up_w, f32).transpose(0, 2, 1)),
        "edwT": np.ascontiguousarray(np.asarray(L0_ffn_expert_down_w, f32).transpose(0, 2, 1)),
        "sgwT": np.ascontiguousarray(np.asarray(L0_ffn_shared_gate_w, f32).T),
        "suwT": np.ascontiguousarray(np.asarray(L0_ffn_shared_up_w, f32).T),
        "sdwT": np.ascontiguousarray(np.asarray(L0_ffn_shared_down_w, f32).T),
    }
    if _NC_CACHE is None:
        _NC_CACHE = build_program()
    nc = _NC_CACHE
    in_maps = [{"x": np.ascontiguousarray(x[i]), **com} for i in range(NCORES)]
    res = run_bass_kernel_spmd(nc, in_maps, core_ids=list(range(NCORES)))
    return np.stack([res.results[i]["out"] for i in range(NCORES)], axis=0)
